# version 36
# baseline (speedup 1.0000x reference)
import math

import numpy as np

import concourse.bass as bass
import concourse.mybir as mybir
from concourse.bass_utils import run_bass_kernel_spmd

F32 = mybir.dt.float32
BF16 = mybir.dt.bfloat16
AF = mybir.ActivationFunctionType
ALU = mybir.AluOpType
AX = mybir.AxisListType
I32 = mybir.dt.int32
import struct as _struct
MAGICF = _struct.unpack('f', _struct.pack('I', 0x5f3759df))[0]

B, T, D = 4, 4096, 1024
NH, HD = 16, 64
TLOC = 2048
NE, EI = 16, 512
SI = 1024
TOPK = 4
SHIFT = 8.0
MAXREL = 16
NKB = T // 128
NPATCH = 17
EPS = 1e-5

ENGINES = ("tensor", "vector", "scalar", "gpsimd", "sync")
DMA_QUEUES = ("sync", "scalar", "gpsimd")


PQW = 1664


def pq_layout():
    out = []
    for uq in range(4):
        lo, hi = uq * 512, uq * 512 + 512
        ents = []
        off = 0
        for kb in range(NPATCH):
            w0, w1 = max(0, kb * 128 - 128), min(TLOC, kb * 128 + 256)
            c0, c1 = max(w0, lo), min(w1, hi)
            if c0 < c1:
                ents.append((kb, c0, c1, off))
                off += c1 - c0
        assert off <= PQW, off
        out.append(ents)
    return out


PQL = pq_layout()


def host_tables(rel_bias):
    rb = np.asarray(rel_bias)
    patches = np.zeros((2, NH, 4, 128, PQW), dtype=np.float32)
    ctab = np.zeros((2, NH, NKB, 2), dtype=np.float32)
    kp = np.arange(128)
    for half in range(2):
        for uq in range(4):
            for (kb, c0, c1, off) in PQL[uq]:
                u = np.arange(c0, c1)
                kprime = kb * 128 + kp
                dist = kprime[:, None] - u[None, :]
                if half == 1 and kb == 16:
                    dist = dist - T
                idx = np.clip(dist, -MAXREL + 1, MAXREL - 1) + MAXREL - 1
                patches[half, :, uq, :, off:off + (c1 - c0)] = np.transpose(
                    rb[idx], (2, 0, 1))
        for kb in range(NKB):
            if kb <= 15 or half == 0:
                ctab[half, :, kb, 0] = rb[2 * MAXREL - 2]
            else:
                ctab[half, :, kb, 0] = rb[0]
            ctab[half, :, kb, 1] = rb[0]
    return patches, ctab


class Sched:

    def __init__(self, nc):
        self.nc = nc
        self.phases = []
        self.cur = {e: [] for e in ENGINES}
        self.eng_idx = {e: 0 for e in ENGINES}
        self.dma_cnt = {q: 0 for q in DMA_QUEUES}
        self.last_w = {}
        self.readers = {}
        self.clock = {e: {} for e in ENGINES}
        self.flag = {e: set() for e in ENGINES}
        self.op_clock = {}
        self.esem = {e: nc.alloc_semaphore(f"es_{e}") for e in ENGINES}
        self.NDSEM = 12
        self.dsem = {(q, i): nc.alloc_semaphore(f"ds_{q}{i}")
                     for q in DMA_QUEUES for i in range(self.NDSEM)}
        self.dsem_total = {k: 0 for k in self.dsem}
        self.dma_slot = {q: 0 for q in DMA_QUEUES}

    def _dep_tokens(self, reads, writes, writes_accum):
        need = []
        for k in reads:
            t = self.last_w.get(k)
            if t is not None:
                need.append(t)
        for k in writes:
            t = self.last_w.get(k)
            if t is not None:
                need.append(t)
            need.extend(self.readers.get(k, ()))
        for k in writes_accum:
            need.extend(self.readers.get(k, ()))
        return need

    def emit(self, eng, fn, reads=(), writes=(), writes_accum=(), dma=False):
        assert eng in ENGINES
        need = self._dep_tokens(reads, writes, writes_accum)
        if dma:
            q = eng
            slot = (q, self.dma_slot[q])
            self.dma_slot[q] = (self.dma_slot[q] + 1) % self.NDSEM
            if self.dsem_total[slot] > 0:
                need = list(need) + [("D", slot, self.dsem_total[slot])]
            self.dsem_total[slot] += 16
            tok = ("D", slot, self.dsem_total[slot])
        else:
            tok = ("E", eng, self.eng_idx[eng])
        waits = []
        cl = self.clock[eng]
        for t in need:
            kind, st_, v = t
            if cl.get((kind, st_), -1) >= v:
                continue
            waits.append(t)
            cl[(kind, st_)] = max(cl.get((kind, st_), -1), v)
            for stx, sv in self.op_clock.get(t, {}).items():
                if cl.get(stx, -1) < sv:
                    cl[stx] = sv
            if kind == "E":
                self.flag[st_].add(v)
        op = dict(eng=eng, fn=fn, waits=waits, tok=tok, dma=dma)
        self.cur[eng].append(op)
        if dma:
            self.dma_cnt[eng] += 16
            op["dsem_slot"] = tok[1]
        else:
            self.eng_idx[eng] += 1
        snap = dict(cl)
        snap[(tok[0], tok[1])] = tok[2]
        self.op_clock[tok] = snap
        for k in reads:
            self.readers.setdefault(k, []).append(tok)
        for k in writes:
            self.last_w[k] = tok
            self.readers[k] = []
        for k in writes_accum:
            self.last_w[k] = tok
        return tok

    def phase(self):
        for q in DMA_QUEUES:
            waits = []
            cl = self.clock[q]
            for slot, tot in self.dsem_total.items():
                if slot[0] == q and tot > 0 and cl.get(("D", slot), -1) < tot:
                    waits.append(("D", slot, tot))
                    cl[("D", slot)] = tot
            if waits:
                self.cur[q].append(dict(eng=q, fn=None, waits=waits, tok=None,
                                        dma=False))
        self.phases.append(self.cur)
        self.cur = {e: [] for e in ENGINES}
        vis = {("E", e): self.eng_idx[e] - 1 for e in ENGINES}
        for slot, tot in self.dsem_total.items():
            if tot > 0:
                vis[("D", slot)] = tot
        for e in ENGINES:
            cl = self.clock[e]
            for st, sv in vis.items():
                if cl.get(st, -1) < sv:
                    cl[st] = sv

    def finish(self):
        if any(self.cur[e] for e in ENGINES):
            self.phase()
        fin = {e: [] for e in ENGINES}
        for slot, tot in self.dsem_total.items():
            if tot > 0:
                fin[slot[0]].append(("D", slot, tot))
        self.phases.append(
            {e: [dict(eng=e, fn=None, waits=fin[e], tok=None, dma=False)]
             if fin[e] else [] for e in ENGINES})
        self._emit()

    def _emit(self):
        nc = self.nc
        pref = {e: sorted(self.flag[e]) for e in ENGINES}

        def semval(e, idx):
            import bisect
            return bisect.bisect_right(pref[e], idx)

        for ph in self.phases:
            if not any(ph[e] for e in ENGINES):
                continue
            with nc.Block() as blk:
                for e in ENGINES:
                    ops = ph[e]

                    def mk(section_eng, section_ops):
                        def body(engobj):
                            for op in section_ops:
                                for kind, st_, v in op["waits"]:
                                    if kind == "E":
                                        engobj.wait_ge(self.esem[st_],
                                                       semval(st_, v))
                                    else:
                                        engobj.wait_ge(self.dsem[st_], v)
                                if op["fn"] is None:
                                    continue
                                instr = op["fn"](engobj)
                                if op["dma"]:
                                    instr.then_inc(self.dsem[op["dsem_slot"]], 16)
                                elif op["tok"][2] in self.flag[op["eng"]]:
                                    instr.then_inc(self.esem[op["eng"]], 1)
                        return body

                    getattr(blk, e)(mk(e, ops))


class MK:
    def __init__(self, dbg=()):
        self.nc = bass.Bass("TRN2", target_bir_lowering=False)
        self.s = Sched(self.nc)
        self.dbg = set(dbg)
        self.dbg_outs = {}
        self._declare_io()

    def _declare_io(self):
        nc = self.nc
        d = {}
        d["xrot"] = nc.dram_tensor("xrot", [T, D], F32, kind="ExternalInput")
        for w in ("Wq", "Wk", "Wv", "Wo"):
            d[w] = nc.dram_tensor(w, [D, D], F32, kind="ExternalInput")
        for v in ("bq", "bk", "bv", "bo"):
            d[v] = nc.dram_tensor(v, [D], F32, kind="ExternalInput")
        d["gate_w"] = nc.dram_tensor("gate_w", [NE, D], F32, kind="ExternalInput")
        d["eg"] = nc.dram_tensor("eg", [NE, 4, 128, 8, 128], F32,
                                 kind="ExternalInput")
        d["eu"] = nc.dram_tensor("eu", [NE, 4, 128, 8, 128], F32,
                                 kind="ExternalInput")
        d["ed"] = nc.dram_tensor("ed", [NE, EI, D], F32, kind="ExternalInput")
        d["sg"] = nc.dram_tensor("sg", [D, SI], F32, kind="ExternalInput")
        d["su"] = nc.dram_tensor("su", [D, SI], F32, kind="ExternalInput")
        d["sd"] = nc.dram_tensor("sd", [SI, D], F32, kind="ExternalInput")
        d["patches"] = nc.dram_tensor("patches", [NH, 4, 128, PQW], F32,
                                      kind="ExternalInput")
        d["ctab"] = nc.dram_tensor("ctab", [NH, NKB, 2], F32, kind="ExternalInput")
        d["out"] = nc.dram_tensor("out", [TLOC, D], F32, kind="ExternalOutput")
        d["xnrd"] = nc.dram_tensor("xnrd", [128, 8 * TLOC], BF16)
        d["ahd"] = nc.dram_tensor("ahd", [128, 8 * TLOC], BF16)
        d["combd"] = nc.dram_tensor("combd", [16, TLOC], F32)
        d["sumsd"] = nc.dram_tensor("sumsd", [16, TLOC], F32)
        d["recd"] = nc.dram_tensor("recd", [16, TLOC], F32)
        self.io = d

    def dbg_out(self, name, shape, dtype=F32):
        t = self.nc.dram_tensor("dbg_" + name, shape, dtype, kind="ExternalOutput")
        self.dbg_outs[name] = t
        return t

    def sb(self, name, shape, dtype):
        return self.nc.alloc_sbuf_tensor(name, shape, dtype)

    def dma(self, q, dst, src, reads=(), writes=(), **kw):
        return self.s.emit(q, lambda e: e.dma_start(dst, src, **kw), reads=reads,
                           writes=writes, dma=True)

    def build(self, upto="full", pairs=range(8)):
        self._alloc_persistent()
        self._p0_startup()
        if upto != "p0":
            self._p2_merged(pairs=pairs)
        if upto not in ("p0", "p2"):
            self._p3_oproj_shared()
            self._p6_final()
        self.s.finish()
        self.nc.finalize()
        return self.nc

    def _alloc_persistent(self):
        nc = self.nc
        self.pbig = nc.alloc_psum_tensor("pbig", [128, 4096], F32)
        self.pbig_bf = self.pbig.bitcast(BF16)
        self.pb = [self.pbig[:, i * 512:(i + 1) * 512] for i in range(8)]
        self.pb_bf = [self.pbig_bf[:, i * 1024:(i + 1) * 1024] for i in range(8)]

        self.xnl = self.sb("xnl", [128, 8 * TLOC], BF16)
        self.identf = self.sb("identf", [128, 128], F32)
        self.identb = self.sb("identb", [128, 128], BF16)
        self.m8 = self.sb("m8", [128, 1], F32)
        self.half = self.sb("half", [128, 1], F32)
        self.onesb = self.sb("onesb", [65, 128], F32)
        self.onesbf = self.sb("onesbf", [65, 128], BF16)
        self.b8 = {}
        for v in ("bq", "bk", "bv", "bo"):
            self.b8[v] = self.sb(v + "8", [128, 8], F32)
        self.ctabB = self.sb("ctabB", [128, NH * NKB * 2], F32)
        self.wqtp = self.sb("wqtp", [128, 8 * 128], BF16)
        self.wktp = self.sb("wktp", [128, 8 * 128], BF16)
        self.wvtp = self.sb("wvtp", [128, 8 * 128], BF16)
        self.wstage = self.sb("wstage", [128, 1024], BF16)

        self.ar_p0 = nc.auto_arena()
        a = self.ar_p0
        self.xt = [a.sbuf_tensor(f"xt{i}", [128, D], F32) for i in range(6)]
        self.sqs = a.sbuf_tensor("sqs", [128, D], F32)
        self.lnb = [a.sbuf_tensor(f"lnb{i}", [128, D], BF16) for i in range(4)]
        self.lnf = [a.sbuf_tensor(f"lnf{i}", [128, D], F32) for i in range(4)]
        self.xtf = a.sbuf_tensor("xtf", [128, 8 * 512], F32)
        self.xrstage = a.sbuf_tensor("xrstage", [128, 8 * 512], BF16)
        self.stat = [a.sbuf_tensor(f"stat{i}", [128, 8], F32) for i in range(4)]
        self.vb = [a.sbuf_tensor(f"vb{i}", [128, 4], F32) for i in range(2)]
        self.nb = [a.sbuf_tensor(f"nb{i}", [128, 4], F32) for i in range(2)]
        self.rb = [a.sbuf_tensor(f"rb{i}", [128, 4], F32) for i in range(2)]
        self.nmsb = [a.sbuf_tensor(f"nmsb{i}", [128, 4], F32) for i in range(2)]
        self.cmag = a.sbuf_tensor("cmag", [128, 1], F32)
        self.rtr = [a.sbuf_tensor(f"rtr_{i}", [128, 16], F32) for i in range(4)]
        self.rtr2 = [a.sbuf_tensor(f"rtr2_{i}", [128, 16], F32) for i in range(4)]
        self.rmax = [a.sbuf_tensor(f"rmax_{i}", [128, 1], F32) for i in range(4)]
        self.rden = [a.sbuf_tensor(f"rden_{i}", [128, 1], F32) for i in range(4)]
        self.comb = [a.sbuf_tensor(f"comb_{i}", [128, 16], F32) for i in range(4)]
        self.comball = a.sbuf_tensor("comball", [128, 256], F32)
        self.ct_row = a.sbuf_tensor("ct_row", [1, NH * NKB * 2], F32)
        self.gw_sb = a.sbuf_tensor("gw_sb", [16, D], F32)
        self.gwT = a.sbuf_tensor("gwT", [128, 8 * 16], F32)
        self.combT = a.sbuf_tensor("combT", [16, TLOC], F32)

        self.ar_c = nc.auto_arena()
        self.acc = self.ar_c.sbuf_tensor("acc", [128, 8 * TLOC], F32)

        self.ar_m = nc.auto_arena()
        a = self.ar_m
        self.qp = a.sbuf_tensor("qp", [128, TLOC], BF16)
        self.kt = a.sbuf_tensor("kt", [128, T], BF16)
        self.vt = a.sbuf_tensor("vt", [128, TLOC], BF16)
        self.va = a.sbuf_tensor("va", [128, NKB * 130], BF16)
        self.pq = [[a.sbuf_tensor(f"pq{h}_{i}", [128, PQW], BF16)
                    for i in range(2)] for h in range(2)]
        self.pt = [a.sbuf_tensor(f"pt{i}", [128, 1024], BF16) for i in range(3)]
        self.srow = a.sbuf_tensor("srow", [1, 512], F32)
        self.srq = [a.sbuf_tensor(f"srq{h}", [128, 4], F32) for h in range(2)]
        self.rbc2 = a.sbuf_tensor("rbc2", [64, 512], F32)

        self.xrbuf = [a.sbuf_tensor(f"xrbuf{i}", [128, 8 * 256], BF16)
                      for i in range(2)]
        self.ahstg = [a.sbuf_tensor(f"ahstg{h}", [64, 512], BF16)
                      for h in range(2)]
        self.wgs = [a.sbuf_tensor(f"wgs{i}", [128, 8 * 128], BF16) for i in range(2)]
        self.wus = [a.sbuf_tensor(f"wus{i}", [128, 8 * 128], BF16) for i in range(2)]
        self.wds = [a.sbuf_tensor(f"wds{i}", [128, 4 * 1024], BF16)
                    for i in range(2)]
        self.he = a.sbuf_tensor("he", [128, 4 * 1024], BF16)
        self.ce = a.sbuf_tensor("ce", [128, TLOC], BF16)
        self.scr5 = [a.sbuf_tensor(f"scr5_{i}", [128, 512], BF16) for i in range(2)]
        self.scr5b = [a.sbuf_tensor(f"scr5b_{i}", [128, 512], BF16)
                      for i in range(2)]

        self.ar_p0.allocate(side="right")

    def _p0_startup(self):
        nc, s = self.nc, self.s
        io = self.io

        s.emit("gpsimd", lambda e: e.iota(self.identf[:], [[1, 128]], base=0,
                                          channel_multiplier=-1,
                                          allow_small_or_imprecise_dtypes=True),
               writes=[("identf",)])
        s.emit("vector", lambda e: e.tensor_scalar(self.identf[:], self.identf[:],
                                                   0.0, None, ALU.is_equal),
               reads=[("identf",)], writes=[("identf",)])
        s.emit("vector", lambda e: e.tensor_copy(self.identb[:], self.identf[:]),
               reads=[("identf",)], writes=[("identb",)])
        s.emit("vector", lambda e: e.memset(self.m8[:], -SHIFT), writes=[("m8",)])
        s.emit("vector", lambda e: e.memset(self.half[:], 0.5), writes=[("half",)])
        s.emit("vector", lambda e: e.memset(self.cmag[:], MAGICF),
               writes=[("cmag",)])
        self.ar_c.allocate()
        for fb in range(8):
            s.emit("vector", lambda e, fb=fb: e.memset(
                self.acc[:, fb * TLOC:(fb + 1) * TLOC], 0.0),
                writes=[("acc", fb)])
        s.emit("vector", lambda e: e.memset(self.onesb[0:1, :], 1.0),
               writes=[("ones",)])
        s.emit("vector", lambda e: e.memset(self.onesb[64:65, :], 1.0),
               writes=[("ones64",)])
        s.emit("vector", lambda e: e.memset(self.onesbf[0:1, :], 1.0),
               writes=[("onesbf",)])
        s.emit("vector", lambda e: e.memset(self.onesbf[64:65, :], 1.0),
               writes=[("onesbf64",)])
        for v in ("bq", "bk", "bv", "bo"):
            dst = self.b8[v]
            src = io[v].rearrange("(a p) -> p a", p=128)
            self.dma("sync", dst[:], src, writes=[("b8", v)],
                     allow_slow_non_contiguous=True)

        ct_row = self.ct_row
        self.dma("sync", ct_row[:], io["ctab"].rearrange("h k r -> (h k r)")[None, :],
                 writes=[("ct_row",)])
        ncol = NH * NKB * 2
        for c0 in range(0, ncol, 512):
            w = min(512, ncol - c0)
            pbk = self.pb[c0 // 512 % 2]

            def mm(e, c0=c0, w=w, pbk=pbk):
                return e.matmul(pbk[:, 0:w], self.onesb[0:1, 0:128],
                                ct_row[:, c0:c0 + w], start=True, stop=True)

            s.emit("tensor", mm, reads=[("ones",), ("ct_row",)],
                   writes=[("pb", c0 // 512 % 2)])
            s.emit("vector", lambda e, c0=c0, w=w, pbk=pbk: e.tensor_scalar(
                self.ctabB[:, c0:c0 + w], pbk[:, 0:w], -SHIFT, None, ALU.add),
                reads=[("pb", c0 // 512 % 2)], writes=[("ctabB",)])

        gw_sb = self.gw_sb
        self.dma("sync", gw_sb[:], io["gate_w"][:], writes=[("gw_sb",)])
        for cb in range(8):
            pbk = self.pb[2 + cb % 2]

            def tr(e, cb=cb, pbk=pbk):
                return e.transpose(pbk[:, 0:16], gw_sb[:, cb * 128:(cb + 1) * 128],
                                   self.identf[0:16, 0:16])

            s.emit("tensor", tr, reads=[("gw_sb",), ("identf",)],
                   writes=[("pb", 2 + cb % 2)])
            s.emit("vector", lambda e, cb=cb, pbk=pbk: e.tensor_copy(
                self.gwT[:, cb * 16:(cb + 1) * 16], pbk[:, 0:16]),
                reads=[("pb", 2 + cb % 2)], writes=[("gwT",)])

        xt, lnb, lnf = self.xt, self.lnb, self.lnf
        xtf = self.xtf

        def ln_a(tl):
            ti = tl % 6
            x = xt[ti]
            st = self.stat[tl % 4]
            sqs = self.sqs
            sk, qk = ("st", tl % 4), ("sqs",)
            s.emit("scalar", lambda e: e.activation(
                sqs[:], x[:], AF.Square, accum_out=st[:, 2:3]),
                reads=[("xt", ti)], writes=[qk, (sk, "sq")])
            s.emit("scalar", lambda e: e.activation(
                sqs[:], x[:], AF.Identity, accum_out=st[:, 0:1]),
                reads=[("xt", ti), qk], writes=[qk, (sk, "sum")])

        def ln_v1(tl):
            st = self.stat[tl % 4]
            sk = ("st", tl % 4)
            g2 = (tl // 4) % 2
            c = tl % 4
            nbc = self.nb[g2][:, c:c + 1]
            vbc = self.vb[g2][:, c:c + 1]
            s.emit("vector", lambda e: e.tensor_scalar(nbc, st[:, 0:1],
                                                       -1.0 / D, None, ALU.mult),
                   reads=[(sk, "sum")], writes=[("nb", g2)])
            s.emit("vector", lambda e: e.tensor_tensor(st[:, 5:6], nbc,
                                                       nbc, ALU.mult),
                   reads=[("nb", g2)], writes=[(sk, "m2")])
            s.emit("vector", lambda e: e.scalar_tensor_tensor(
                vbc, st[:, 2:3], 1.0 / D, st[:, 5:6], ALU.mult, ALU.subtract),
                reads=[(sk, "sq"), (sk, "m2")], writes=[("vb", g2)])
            s.emit("vector", lambda e: e.tensor_scalar(vbc, vbc, EPS,
                                                       None, ALU.add),
                   reads=[("vb", g2)], writes=[("vb", g2)])

        def rsqrt_group(g):
            g2 = g % 2
            vbv, rbv = self.vb[g2], self.rb[g2]
            tmp = self.nmsb[g2]
            vbi, rbi = vbv.bitcast(I32), rbv.bitcast(I32)
            cmagi = self.cmag.bitcast(I32)
            s.emit("vector", lambda e: e.tensor_scalar(
                rbi[:], vbi[:], 1, None, ALU.arith_shift_right),
                reads=[("vb", g2)], writes=[("rb", g2)])
            s.emit("vector", lambda e: e.tensor_tensor(
                rbi[:], cmagi[:].to_broadcast([128, 4]), rbi[:], ALU.subtract),
                reads=[("cmag",), ("rb", g2)], writes=[("rb", g2)])
            for _ in range(3):
                s.emit("vector", lambda e: e.tensor_tensor(tmp[:], rbv[:], rbv[:],
                                                           ALU.mult),
                       reads=[("rb", g2)], writes=[("nmsb", g2)])
                s.emit("vector", lambda e: e.tensor_tensor(tmp[:], tmp[:], vbv[:],
                                                           ALU.mult),
                       reads=[("nmsb", g2), ("vb", g2)], writes=[("nmsb", g2)])
                s.emit("vector", lambda e: e.tensor_scalar(tmp[:], tmp[:], -0.5,
                                                           1.5, ALU.mult, ALU.add),
                       reads=[("nmsb", g2)], writes=[("nmsb", g2)])
                s.emit("vector", lambda e: e.tensor_tensor(rbv[:], rbv[:], tmp[:],
                                                           ALU.mult),
                       reads=[("rb", g2), ("nmsb", g2)], writes=[("rb", g2)])
            s.emit("vector", lambda e: e.tensor_tensor(tmp[:], self.nb[g2][:],
                                                       rbv[:], ALU.mult),
                   reads=[("nb", g2), ("rb", g2)], writes=[("nmsb", g2)])

        def ln_s2(tl):
            ti = tl % 6
            local = tl < 16
            dst = lnf[tl % 4] if local else lnb[tl % 4]
            key = ("lnf", tl % 4) if local else ("lnb", tl % 4)
            g2 = (tl // 4) % 2
            c = tl % 4
            s.emit("scalar", lambda e, dst=dst: e.activation(
                dst[:], xt[ti][:], AF.Identity,
                bias=self.nmsb[g2][:, c:c + 1], scale=self.rb[g2][:, c:c + 1]),
                reads=[("xt", ti), ("nmsb", g2), ("rb", g2)], writes=[key])

        def emit_dma(tl):
            ti = tl % 6
            rows = tl * 128
            self.dma("sync", xt[ti][:], io["xrot"][rows:rows + 128, :],
                     writes=[("xt", ti)])

        def post_tile(tl):
            ti = tl % 4
            local = tl < 16
            if local:
                for cb in range(8):
                    pbi = (tl * 8 + cb) % 2
                    pbk = self.pb[pbi]

                    def tr(e, cb=cb, ti=ti, pbk=pbk):
                        return e.transpose(pbk[:, 0:128],
                                           lnf[ti][:, cb * 128:(cb + 1) * 128],
                                           self.identf[:])

                    s.emit("tensor", tr, reads=[("lnf", ti), ("identf",)],
                           writes=[("pb", pbi)])
                    s.emit("scalar", lambda e, cb=cb, ti=ti, pbk=pbk:
                           e.activation(xtf[:, cb * 512 + ti * 128:
                                            cb * 512 + ti * 128 + 128],
                                        pbk[:, 0:128], AF.Identity),
                           reads=[("pb", pbi)], writes=[("xtf", cb)])
                rp = tl % 4
                pbi = 2 + tl % 2
                pbk = self.pb[pbi]
                for cb in range(8):
                    def mm(e, cb=cb, ti=ti, pbk=pbk):
                        return e.matmul(pbk[:, 0:16],
                                        xtf[:, cb * 512 + ti * 128:
                                            cb * 512 + ti * 128 + 128],
                                        self.gwT[:, cb * 16:(cb + 1) * 16],
                                        start=(cb == 0), stop=(cb == 7))

                    s.emit("tensor", mm, reads=[("xtf", cb), ("gwT",)],
                           writes=[("pb", pbi)] if cb == 0 else (),
                           writes_accum=[("pb", pbi)] if cb > 0 else ())
                s.emit("scalar", lambda e, pbk=pbk, rp=rp: e.activation(
                    self.rtr[rp][:], pbk[:, 0:16], AF.Sigmoid),
                    reads=[("pb", pbi)], writes=[("rtr", rp)])
            else:
                for cb in range(8):
                    pbi = (tl * 8 + cb) % 2
                    pbk = self.pb_bf[pbi]

                    def tr(e, cb=cb, ti=ti, pbk=pbk):
                        return e.transpose(pbk[:, 0:128],
                                           lnb[ti][:, cb * 128:(cb + 1) * 128],
                                           self.identb[:])

                    s.emit("tensor", tr, reads=[("lnb", ti), ("identb",)],
                           writes=[("pb", pbi)])
                    s.emit("vector", lambda e, cb=cb, ti=ti, pbk=pbk:
                           e.tensor_copy(
                        self.xrstage[:, cb * 512 + ti * 128:
                                     cb * 512 + ti * 128 + 128],
                        pbk[:, 0:128]),
                        reads=[("pb", pbi)], writes=[("xrs", cb)])

        def routing_vec(tl):
            rp = tl % 4
            rtr, rtr2 = self.rtr[rp], self.rtr2[rp]
            rmax, rden, comb = self.rmax[rp], self.rden[rp], self.comb[rp]
            s.emit("vector", lambda e: e.tensor_copy(rtr2[:], rtr[:]),
                   reads=[("rtr", rp)], writes=[("rtr2", rp)])
            for it in range(TOPK):
                s.emit("vector", lambda e: e.reduce_max(rmax[:], rtr2[:],
                                                        axis=AX.X),
                       reads=[("rtr2", rp)], writes=[("rmax", rp)])
                if it < TOPK - 1:
                    s.emit("vector", lambda e: e.tensor_scalar(
                        comb[:], rtr2[:], rmax[:], None, ALU.is_ge),
                        reads=[("rtr2", rp), ("rmax", rp)], writes=[("comb", rp)])
                    s.emit("vector", lambda e: e.scalar_tensor_tensor(
                        rtr2[:], comb[:], -1000.0, rtr2[:], ALU.mult,
                        ALU.add),
                        reads=[("comb", rp), ("rtr2", rp)], writes=[("rtr2", rp)])
            s.emit("vector", lambda e: e.tensor_scalar(
                comb[:], rtr[:], rmax[:], None, ALU.is_ge),
                reads=[("rtr", rp), ("rmax", rp)], writes=[("comb", rp)])
            s.emit("vector", lambda e: e.tensor_tensor(comb[:], comb[:],
                                                       rtr[:], ALU.mult),
                   reads=[("comb", rp), ("rtr", rp)], writes=[("comb", rp)])
            s.emit("vector", lambda e: e.reduce_sum(rden[:], comb[:],
                                                    axis=AX.X),
                   reads=[("comb", rp)], writes=[("rden", rp)])
            s.emit("vector", lambda e: e.tensor_scalar(rden[:], rden[:],
                                                       1e-20, None, ALU.add),
                   reads=[("rden", rp)], writes=[("rden", rp)])
            s.emit("vector", lambda e: e.reciprocal(rden[:], rden[:]),
                   reads=[("rden", rp)], writes=[("rden", rp)])
            s.emit("vector", lambda e: e.tensor_scalar(
                self.comball[:, tl * 16:tl * 16 + 16], comb[:], rden[:],
                None, ALU.mult),
                reads=[("comb", rp), ("rden", rp)], writes=[("comball", tl)])

        def chunk_close(ch):
            tbase = (ch % 4) * 512
            if ch < 4:
                for cb in range(8):
                    s.emit("vector", lambda e, cb=cb, tbase=tbase: e.tensor_copy(
                        self.xnl[:, cb * TLOC + tbase:cb * TLOC + tbase + 512],
                        xtf[:, cb * 512:(cb + 1) * 512]),
                        reads=[("xtf", cb)], writes=[("xn", cb)])
            else:
                dst = io["xnrd"].rearrange("p (cb t) -> p cb t", t=TLOC)[
                    :, :, tbase:tbase + 512]
                src = self.xrstage.rearrange("p (cb t) -> p cb t", t=512)
                self.dma("sync", dst, src[:],
                         reads=[("xrs", cb) for cb in range(8)])

        NTL = 32
        emit_dma(0)
        emit_dma(1)
        for t in range(NTL + 6):
            if 4 <= t <= NTL + 3:
                ln_s2(t - 4)
                post_tile(t - 4)
                if (t - 4) % 4 == 3:
                    chunk_close((t - 4) // 4)
            if t < NTL:
                if t + 2 < NTL:
                    emit_dma(t + 2)
                ln_a(t)
                ln_v1(t)
                if t % 4 == 3:
                    rsqrt_group(t // 4)
            if 6 <= t <= NTL + 5 and t - 6 < 16:
                routing_vec(t - 6)
        self.emit_wdmas(0)
        for idx in range(16):
            pbt = self.pb[4 + idx % 2]
            s.emit("tensor", lambda e, idx=idx, pbt=pbt: e.transpose(
                pbt[0:16, 0:128], self.comball[:, idx * 16:idx * 16 + 16],
                self.identf[:]),
                reads=[("comball", idx), ("identf",)], writes=[("pb", 4 + idx % 2)])
            s.emit("vector", lambda e, idx=idx, pbt=pbt: e.tensor_copy(
                self.combT[:, idx * 128:idx * 128 + 128], pbt[0:16, 0:128]),
                reads=[("pb", 4 + idx % 2)], writes=[("combT",)])
        self.dma("sync", io["combd"][:], self.combT[:], reads=[("combT",)])
        if "xnl" in self.dbg:
            t = self.dbg_out("xnl", [128, 8 * TLOC], BF16)
            self.dma("sync", t[:], self.xnl[:],
                     reads=[("xn", cb) for cb in range(8)])
        if "combT" in self.dbg:
            t = self.dbg_out("combT", [16, TLOC])
            self.dma("sync", t[:], self.combT[:], reads=[("combT",)])
        self.s.phase()
        self.ar_p0.free()

    def emit_wdmas(self, hp):
        s, io = self.s, self.io
        for srcname, dst, key in (("Wq", self.wqtp, ("wqtp",)),
                                  ("Wk", self.wktp, ("wktp",)),
                                  ("Wv", self.wvtp, ("wvtp",))):
            src = io[srcname][hp * 128:(hp + 1) * 128, :]
            self.dma("gpsimd", self.wstage[:], src,
                     reads=[], writes=[("wstage",)])
            dst3 = dst.rearrange("p (cb f) -> p cb f", f=128)
            src3 = self.wstage.rearrange("p (cb f) -> p cb f", f=128)
            self.dma("sync", dst3[:], src3[:], reads=[("wstage",)],
                     writes=[key], transpose=True)

    def _pair_setup(self, hp, step):
        s, io = self.s, self.io
        for c in range(2):
            xb = self.xrbuf[c % 2]
            src = io["xnrd"].rearrange("p (cb t) -> p cb t", t=TLOC)[
                :, :, c * 256:(c + 1) * 256]
            self.dma("sync", xb.rearrange("p (cb t) -> p cb t", t=256)[:],
                     src, writes=[("xrb", c % 2, 0), ("xrb", c % 2, 1)])
        va3 = self.va.rearrange("p (kb c) -> p kb c", c=130)
        s.emit("vector", lambda e: e.memset(va3[:, :, 64:65], 1.0),
               writes=[("va", "ones")])
        s.emit("vector", lambda e: e.memset(va3[:, :, 129:130], 1.0),
               writes=[("va", "ones2")])

        def proj_local(dst, dkey, wt, wkey, bias, scale, c):
            pbi = c % 2
            pbk = self.pb[pbi]
            for cb in range(8):
                def mm(e, cb=cb, pbk=pbk, c=c, wt=wt):
                    return e.matmul(pbk[:], wt[:, cb * 128:(cb + 1) * 128],
                                    self.xnl[:, cb * TLOC + c * 512:
                                             cb * TLOC + c * 512 + 512],
                                    start=(cb == 0), stop=(cb == 7))

                s.emit("tensor", mm, reads=[wkey, ("xn", cb)],
                       writes=[("pb", pbi)] if cb == 0 else (),
                       writes_accum=[("pb", pbi)] if cb > 0 else ())
            if scale is None:
                s.emit("vector", lambda e, c=c, pbk=pbk, dst=dst: e.tensor_scalar(
                    dst[:, c * 512:(c + 1) * 512], pbk[:], bias, None, ALU.add),
                    reads=[("pb", pbi)], writes=[dkey])
            else:
                s.emit("vector", lambda e, c=c, pbk=pbk, dst=dst: e.tensor_scalar(
                    dst[:, c * 512:(c + 1) * 512], pbk[:], bias, scale,
                    ALU.add, ALU.mult),
                    reads=[("pb", pbi)], writes=[dkey])

        for c in range(4):
            proj_local(self.qp, ("qp",), self.wqtp, ("wqtp",),
                       self.b8["bq"][:, hp:hp + 1], 0.125, c)
        for c in range(4):
            proj_local(self.kt, ("kt",), self.wktp, ("wktp",),
                       self.b8["bk"][:, hp:hp + 1], None, c)
        for c in range(4):
            proj_local(self.vt, ("vt",), self.wvtp, ("wvtp",),
                       self.b8["bv"][:, hp:hp + 1], None, c)
        self._v_transpose(0)
        for c in range(8):
            par = c % 2
            xb = self.xrbuf[par]
            if c >= 2:
                src = io["xnrd"].rearrange("p (cb t) -> p cb t", t=TLOC)[
                    :, :, c * 256:(c + 1) * 256]
                dst = xb.rearrange("p (cb t) -> p cb t", t=256)
                self.dma("sync", dst[:, 0:4], src[:, 0:4],
                         writes=[("xrb", par, 0)])
                self.dma("sync", dst[:, 4:8], src[:, 4:8],
                         writes=[("xrb", par, 1)])
            for which in ("k", "v"):
                pbi = 2 + c % 2
                pbk = self.pb[pbi]
                wt, wkey = ((self.wktp, ("wktp",)) if which == "k"
                            else (self.wvtp, ("wvtp",)))
                bias = (self.b8["bk"] if which == "k" else
                        self.b8["bv"])[:, hp:hp + 1]
                for cb in range(8):
                    def mm(e, cb=cb, pbk=pbk, xb=xb, wt=wt):
                        return e.matmul(pbk[:, 0:256],
                                        wt[:, cb * 128:(cb + 1) * 128],
                                        xb[:, cb * 256:(cb + 1) * 256],
                                        start=(cb == 0), stop=(cb == 7))

                    s.emit("tensor", mm,
                           reads=[wkey, ("xrb", par, cb // 4)],
                           writes=[("pb", pbi)] if cb == 0 else (),
                           writes_accum=[("pb", pbi)] if cb > 0 else ())
                if which == "k":
                    col = 2048 + c * 256
                    wr = [("ktr",)]
                    dstt = self.kt
                else:
                    col = c * 256
                    wr = [("vt",), ("vtr",)]
                    dstt = self.vt

                def ev(e, col=col, pbk=pbk, dstt=dstt, bias=bias):
                    return e.tensor_scalar(dstt[:, col:col + 256], pbk[:, 0:256],
                                           bias, None, ALU.add)

                s.emit("vector", ev, reads=[("pb", pbi)], writes=wr)
        self._v_transpose(1)

    def _v_transpose(self, vh):
        s = self.s
        va3 = self.va.rearrange("p (kb c) -> p kb c", c=130)
        vkey = ("vt",) if vh == 0 else ("vtr",)
        for kk in range(16):
            kb = vh * 16 + kk
            pbi = 2 + kk % 2
            pbk = self.pb_bf[pbi]
            s.emit("tensor", lambda e, kk=kk, pbk=pbk: e.transpose(
                pbk[:, 0:128], self.vt[:, kk * 128:(kk + 1) * 128], self.identb[:]),
                reads=[vkey, ("identb",)], writes=[("pb", pbi)])
            s.emit("vector", lambda e, kb=kb, pbk=pbk, va3=va3: e.tensor_copy(
                va3[:, kb, 0:64], pbk[:, 0:64]),
                reads=[("pb", pbi)], writes=[("va", kb, 0)])
            s.emit("vector", lambda e, kb=kb, pbk=pbk, va3=va3: e.tensor_copy(
                va3[:, kb, 65:129], pbk[:, 64:128]),
                reads=[("pb", pbi)], writes=[("va", kb, 1)])

    def _expert_units(self, e):
        s, io = self.s, self.io
        par = e % 2
        wd = self.wds[par]
        rot = [0]

        def dma_wd():
            src = io["ed"][e].rearrange("(ib p) f -> p ib f", p=128)
            dst3 = wd.rearrange("p (ib f) -> p ib f", ib=4)
            self.dma("gpsimd", dst3[:], src, writes=[("wd", par)])

        def dma_gu(ib):
            spar = ib % 2
            for mat in ("eg", "eu"):
                src = io[mat][e, ib]
                buf = (self.wgs if mat == "eg" else self.wus)[spar]
                dst3 = buf.rearrange("p (cb i) -> p cb i", cb=8)
                self.dma("gpsimd", dst3[:], src, writes=[(mat, spar)])

        def build_ce():
            src = bass.AP(io["combd"], e * TLOC, [[0, 128], [1, TLOC]])
            self.dma("gpsimd", self.ce[:], src, writes=[("ce",)])
            s.emit("vector", lambda e_: e_.tensor_scalar(
                self.ce[:], self.ce[:], 0.5, None, ALU.mult),
                reads=[("ce",)], writes=[("ce",)])

        def g_unit(ib, cc, uh, mat):
            spar = ib % 2
            pbi = 4 + 2 * (rot[0] % 2) + (0 if mat == "eg" else 1)
            uc = uh * 1024 + cc * 512
            buf = (self.wgs if mat == "eg" else self.wus)[spar]
            pbk = self.pb[pbi]
            for cb in range(8):
                def mmg(e_, cb=cb, buf=buf, uc=uc, pbk=pbk):
                    return e_.matmul(
                        pbk[:], buf[:, cb * 128:(cb + 1) * 128],
                        self.xnl[:, cb * TLOC + uc:cb * TLOC + uc + 512],
                        start=(cb == 0), stop=(cb == 7))

                s.emit("tensor", mmg, reads=[(mat, spar)],
                       writes=[("pb", pbi)] if cb == 0 else (),
                       writes_accum=[("pb", pbi)] if cb > 0 else ())
            return pbi

        def gu_epilogue(ib, cc, uh, bg, bu):
            pr = rot[0] % 2
            rot[0] += 1
            sc1, sc2 = self.scr5[pr], self.scr5b[pr]
            uc = uh * 1024 + cc * 512
            s.emit("scalar", lambda e_, bg=bg, sc1=sc1: e_.activation(
                sc1[:], self.pb[bg][:], AF.Tanh, scale=self.half[:]),
                reads=[("pb", bg), ("half",)], writes=[("scr5", pr)])
            s.emit("vector", lambda e_, bg=bg, sc1=sc1, sc2=sc2:
                   e_.scalar_tensor_tensor(
                sc2[:], sc1[:], 1.0, self.pb[bg][:], ALU.add, ALU.mult),
                reads=[("scr5", pr), ("pb", bg)], writes=[("scr5b", pr)])
            s.emit("vector", lambda e_, bu=bu, sc2=sc2: e_.tensor_tensor(
                sc2[:], sc2[:], self.pb[bu][:], ALU.mult),
                reads=[("scr5b", pr), ("pb", bu)], writes=[("scr5b", pr)])
            s.emit("vector", lambda e_, ib=ib, cc=cc, uc=uc, sc2=sc2:
                   e_.tensor_tensor(
                self.he[:, ib * 1024 + cc * 512:ib * 1024 + cc * 512 + 512],
                sc2[:], self.ce[:, uc:uc + 512], ALU.mult),
                reads=[("scr5b", pr), ("ce",)], writes=[("he", ib, cc)])

        def down_unit(fb, cc, uh):
            pbi = 4 + rot[0] % 4
            rot[0] += 1
            pbk = self.pb[pbi]
            for ib in range(4):
                def mmd(e_, fb=fb, cc=cc, ib=ib, pbk=pbk):
                    return e_.matmul(
                        pbk[:], wd[:, ib * 1024 + fb * 128:
                                   ib * 1024 + fb * 128 + 128],
                        self.he[:, ib * 1024 + cc * 512:
                                ib * 1024 + cc * 512 + 512],
                        start=(ib == 0), stop=(ib == 3))

                s.emit("tensor", mmd, reads=[("wd", par), ("he", ib, cc)],
                       writes=[("pb", pbi)] if ib == 0 else (),
                       writes_accum=[("pb", pbi)] if ib > 0 else ())
            uc = uh * 1024 + cc * 512

            def dadd(e_, fb=fb, uc=uc, pbk=pbk):
                return e_.tensor_tensor(
                    self.acc[:, fb * TLOC + uc:fb * TLOC + uc + 512],
                    self.acc[:, fb * TLOC + uc:fb * TLOC + uc + 512],
                    pbk[:], ALU.add)

            s.emit("vector", dadd, reads=[("pb", pbi), ("acc", fb)],
                   writes=[("acc", fb)])

        yield lambda: (dma_wd(), dma_gu(0))
        yield build_ce

        def gu_pair(ib, cc, uh, pre=None):
            def first():
                if pre is not None:
                    pre()
                g_unit(ib, cc, uh, "eg")

            def second():
                bu = g_unit(ib, cc, uh, "eu")
                gu_epilogue(ib, cc, uh, bu - 1, bu)
            return first, second

        for uh in range(2):
            for ib in range(4):
                nxt = ib + 1
                if nxt < 4:
                    pre = (lambda nxt=nxt: dma_gu(nxt))
                elif uh == 0:
                    pre = (lambda: dma_gu(0))
                else:
                    pre = None
                for cc in range(2):
                    f1, f2 = gu_pair(ib, cc, uh, pre if cc == 0 else None)
                    yield f1
                    yield f2
            for fb in range(8):
                for cc in range(2):
                    yield lambda fb=fb, cc=cc, uh=uh: down_unit(fb, cc, uh)

    def _load_pq(self, h, uq, slot):
        h01 = h % 2
        buf = self.pq[h01][slot]
        src = self.io["patches"][h, uq]
        self.dma("gpsimd", buf[:], src, writes=[("pq", h01, slot)])

    def _pair_heads(self, hp, step, next_hp):
        s, io = self.s, self.io
        h0, h1 = 2 * hp, 2 * hp + 1
        va3 = self.va.rearrange("p (kb c) -> p kb c", c=130)
        for uq in range(4):
            u0 = uq * 512
            slot = uq % 2
            if not self._pq0_done:
                self._load_pq(h0, 0, 0)
                self._load_pq(h1, 0, 0)
                self._pq0_done = True
            pql = PQL[uq]
            pqkb = {kb: (c0, c1, off) for (kb, c0, c1, off) in pql}

            def emit_av(kb):
                par = kb % 3
                ptt = self.pt[par]
                for h01 in range(2):
                    bank = 2 + h01

                    def mv(e, kb=kb, h01=h01, ptt=ptt):
                        pc = 1024 + h01 * 512
                        return e.matmul(self.pbig[0:65, pc:pc + 512],
                                        va3[:, kb, 65 * h01:65 * h01 + 65],
                                        ptt[:, h01 * 512:h01 * 512 + 512],
                                        start=(kb == 0), stop=(kb == NKB - 1))

                    s.emit("tensor", mv,
                           reads=[("va", kb, h01), ("va", "ones"), ("va", "ones2"),
                                  ("pt", par)],
                           writes=[("pb", 2 + h01)] if kb == 0 else (),
                           writes_accum=[("pb", 2 + h01)] if kb > 0 else ())

            for kb in range(NKB):
                if kb in (6, 12, 14, 20) and self._norm_q:
                    self._norm_q.popleft()()
                if kb == 20:
                    nuq = uq + 1
                    if nuq < 4:
                        self._load_pq(h0, nuq, (uq + 1) % 2)
                        self._load_pq(h1, nuq, (uq + 1) % 2)
                    elif next_hp is not None:
                        self._load_pq(2 * next_hp, 0, (uq + 1) % 2)
                        self._load_pq(2 * next_hp + 1, 0, (uq + 1) % 2)
                par = kb % 3
                ptt = self.pt[par]
                if kb <= NPATCH - 1:
                    w0g = kb * 128 - 128
                    w0, w1 = max(0, w0g), min(TLOC, kb * 128 + 256)
                else:
                    w0 = w1 = TLOC
                drained = getattr(step, "drained", False)
                bias_emits = {}
                for h01 in range(2):
                    h = 2 * hp + h01
                    rbase = 64 * h01
                    sbank = (6 + h01) if (drained and kb % 2) else h01
                    scol = sbank * 512

                    def mm(e, kb=kb, rbase=rbase, u0=u0, scol=scol, h01=h01):
                        return e.matmul(
                            self.pbig[:, scol:scol + 512],
                            self.kt[rbase:rbase + 64, kb * 128:(kb + 1) * 128],
                            self.qp[rbase:rbase + 64, u0:u0 + 512],
                            start=True, stop=True,
                            tile_position=(rbase, 0))

                    s.emit("tensor", mm, reads=[("kt",), ("ktr",), ("qp",)],
                           writes=[("pb", sbank)])
                for h01 in range(2):
                    h = 2 * hp + h01
                    sbank = (6 + h01) if (drained and kb % 2) else h01
                    scol = sbank * 512
                    bias_emits[h01] = []
                    cta = self.ctabB[:, (h * NKB + kb) * 2:
                                     (h * NKB + kb) * 2 + 1]
                    ctb = self.ctabB[:, (h * NKB + kb) * 2 + 1:
                                     (h * NKB + kb) * 2 + 2]
                    regions = []
                    ent = pqkb.get(kb) if kb <= NPATCH - 1 else None
                    a = b = None
                    if ent is not None:
                        c0, c1, off = ent
                        a, b = c0, c1
                    if w0 > u0:
                        regions.append((u0, min(w0, u0 + 512), cta, None))
                    if a is not None and a < b:
                        regions.append((a, b, self.m8[:], (off, a)))
                    if w1 < u0 + 512:
                        regions.append((max(w1, u0), u0 + 512, ctb, None))
                    for (r0, r1, bias, pofs) in regions:
                        if pofs is not None:
                            off0, astart = pofs

                            def padd(e, r0=r0, r1=r1, off0=off0, u0=u0,
                                     scol=scol, h01=h01, slot=slot):
                                pc = scol + (r0 - u0)
                                return e.tensor_tensor(
                                    self.pbig[:, pc:pc + (r1 - r0)],
                                    self.pbig[:, pc:pc + (r1 - r0)],
                                    self.pq[h01][slot][:, off0:off0 + (r1 - r0)],
                                    ALU.add)

                            bias_emits[h01].append(
                                ("vector", padd,
                                 [("pb", sbank), ("pq", h01, slot)],
                                 [("pb", sbank)]))

                        def ex(e, r0=r0, r1=r1, bias=bias, u0=u0, scol=scol,
                               h01=h01, ptt=ptt):
                            pc = r0 - u0
                            return e.activation(
                                ptt[:, h01 * 512 + pc:h01 * 512 + pc + (r1 - r0)],
                                self.pbig[:, scol + pc:scol + pc + (r1 - r0)],
                                AF.Exp, bias=bias)

                        bias_emits[h01].append(
                            ("scalar", ex,
                             [("pb", sbank), ("ctabB",), ("m8",)],
                             [("pt", par)]))
                for h01 in ((0, 1) if kb % 2 == 0 else (1, 0)):
                    for eng, fn, rd, wr in bias_emits[h01]:
                        s.emit(eng, fn, reads=rd, writes=wr)
                if kb > 1:
                    emit_av(kb - 2)
                step()
            emit_av(NKB - 2)
            emit_av(NKB - 1)
            step()
            for h01 in range(2):
                h = 2 * hp + h01
                s.emit("vector", lambda e, h01=h01: e.tensor_copy(
                    self.srow[:],
                    self.pbig[64:65, 1024 + h01 * 512:1536 + h01 * 512]),
                    reads=[("pb", 2 + h01), ("srow", "d")], writes=[("srow",)])
                self.dma("sync", io["sumsd"][h:h + 1, u0:u0 + 512], self.srow[:],
                         reads=[("srow",)], writes=[("srow", "d"), ("sumd", h01)])
                self.dma("sync", self.srq[h01][:],
                         bass.AP(io["sumsd"], h * TLOC + u0, [[4, 128], [1, 4]]),
                         reads=[("sumd", h01)], writes=[("srq", h01)])

                def ev(e, h01=h01):
                    return e.tensor_copy(
                        self.ahstg[h01][:],
                        self.pbig[0:64, 1024 + h01 * 512:1536 + h01 * 512])

                s.emit("vector", ev, reads=[("pb", 2 + h01), ("ahd", h01)],
                       writes=[("ahstg", h01)])

                def norm_g1(h01=h01, h=h, u0=u0):
                    s.emit("vector", lambda e: e.reciprocal(
                        self.srq[h01][:], self.srq[h01][:]),
                        reads=[("srq", h01)], writes=[("srq", h01)])
                    self.dma("sync",
                             bass.AP(io["recd"], h * TLOC + u0,
                                     [[4, 128], [1, 4]]),
                             self.srq[h01][:], reads=[("srq", h01)],
                             writes=[("recq", h01)])
                    self.dma("sync", self.rbc2[:],
                             bass.AP(io["recd"], h * TLOC + u0,
                                     [[0, 64], [1, 512]]),
                             reads=[("recq", h01), ("rbc2", "rd")],
                             writes=[("rbc2",)])

                def norm_g2(h01=h01, hp=hp, u0=u0):
                    s.emit("vector", lambda e: e.tensor_tensor(
                        self.ahstg[h01][:], self.ahstg[h01][:],
                        self.rbc2[:], ALU.mult),
                        reads=[("ahstg", h01), ("rbc2",)],
                        writes=[("ahstg", h01), ("rbc2", "rd")])
                    self.dma("sync",
                             io["ahd"][64 * h01:64 * h01 + 64,
                                       hp * TLOC + u0:hp * TLOC + u0 + 512],
                             self.ahstg[h01][:],
                             reads=[("ahstg", h01)], writes=[("ahd", h01)])

                self._norm_q.append(norm_g1)
                self._norm_q.append(norm_g2)

    def _p2_merged(self, pairs=range(8)):
        s = self.s
        self.ar_m.allocate()
        pairs = list(pairs)
        import itertools
        gens = {hp: itertools.chain(self._expert_units(2 * hp),
                                    self._expert_units(2 * hp + 1))
                for hp in pairs}

        NUNITS = 132
        NSLOTS = 122

        def make_step(gen):
            st = dict(slot=0, done=0)

            def step(k=1):
                if k > 1:
                    n = k
                else:
                    st["slot"] += 1
                    n = (st["slot"] * NUNITS) // NSLOTS - st["done"]
                for _ in range(max(0, n)):
                    try:
                        u = next(gen)
                    except StopIteration:
                        step.drained = True
                        return
                    u()
                    st["done"] += 1
            step.drained = False
            return step

        self._pq0_done = False
        import collections
        self._norm_q = collections.deque()
        PREPULL = 12
        for i, hp in enumerate(pairs):
            g = gens[hp]
            npre = 0
            for _ in range(PREPULL):
                try:
                    u = next(g)
                except StopIteration:
                    break
                u()
                npre += 1
            self._pair_setup(hp, lambda: None)
            if i + 1 < len(pairs):
                self.emit_wdmas(pairs[i + 1])
            nxt = pairs[i + 1] if i + 1 < len(pairs) else None
            step = make_step(gens[hp])
            self._pair_heads(hp, step, nxt)
            step(200)
        while self._norm_q:
            self._norm_q.popleft()()
        self.s.phase()
        self.ar_m.free()

    def _p3_oproj_shared(self):
        nc, s, io = self.nc, self.s, self.io
        ar = nc.auto_arena()
        ahall = ar.sbuf_tensor("ahall", [128, 8 * TLOC], BF16)
        wot = ar.sbuf_tensor("wot", [128, 8 * 1024], BF16)
        stage2 = ar.sbuf_tensor("stage2", [128, 8 * 1024], BF16)
        ar.allocate()
        src = io["Wo"].rearrange("(fb p) c -> p fb c", p=128)
        dst3 = stage2.rearrange("p (fb c) -> p fb c", c=1024)
        self.dma("gpsimd", dst3[:], src, writes=[("stage2",)])
        for cb in range(8):
            self.dma("sync", ahall[:, cb * TLOC:(cb + 1) * TLOC],
                     io["ahd"][:, cb * TLOC:(cb + 1) * TLOC],
                     writes=[("ahall", cb)])
        for cb in range(8):
            for fb in range(8):
                pbi = (cb * 8 + fb) % 2
                pbk = self.pb_bf[pbi]

                def tr(e, cb=cb, fb=fb, pbk=pbk):
                    return e.transpose(pbk[:, 0:128],
                                       stage2[:, fb * 1024 + cb * 128:
                                              fb * 1024 + cb * 128 + 128],
                                       self.identb[:])

                s.emit("tensor", tr, reads=[("stage2",), ("identb",)],
                       writes=[("pb", pbi)])
                s.emit("vector", lambda e, cb=cb, fb=fb, pbk=pbk: e.tensor_copy(
                    wot[:, cb * 1024 + fb * 128:cb * 1024 + fb * 128 + 128],
                    pbk[:, 0:128]),
                    reads=[("pb", pbi)], writes=[("wot", cb)])
        for fb in range(8):
            for c in range(4):
                pbi = 2 + (fb * 4 + c) % 2
                pbk = self.pb[pbi]
                for cb in range(8):
                    def mm(e, cb=cb, fb=fb, c=c, pbk=pbk):
                        return e.matmul(
                            pbk[:], wot[:, cb * 1024 + fb * 128:
                                        cb * 1024 + fb * 128 + 128],
                            ahall[:, cb * TLOC + c * 512:cb * TLOC + c * 512 + 512],
                            start=(cb == 0), stop=(cb == 7))

                    s.emit("tensor", mm,
                           reads=[("wot", cb), ("ahall", cb)],
                           writes=[("pb", pbi)] if cb == 0 else (),
                           writes_accum=[("pb", pbi)] if cb > 0 else ())
                s.emit("vector", lambda e, fb=fb, c=c, pbk=pbk:
                       e.scalar_tensor_tensor(
                    self.acc[:, fb * TLOC + c * 512:fb * TLOC + c * 512 + 512],
                    pbk[:], self.b8["bo"][:, fb:fb + 1],
                    self.acc[:, fb * TLOC + c * 512:fb * TLOC + c * 512 + 512],
                    ALU.add, ALU.add),
                    reads=[("pb", pbi), ("acc", fb)], writes=[("acc", fb)])
        self.s.phase()
        ar.free()
        ar2 = nc.auto_arena()
        sgb = ar2.sbuf_tensor("sgb", [128, 8 * 1024], BF16)
        sub = ar2.sbuf_tensor("sub", [128, 8 * 1024], BF16)
        sdb = ar2.sbuf_tensor("sdb", [128, 8 * 1024], BF16)
        hsh = ar2.sbuf_tensor("hsh", [128, 8 * 1024], BF16)
        scr = ar2.sbuf_tensor("scr4", [128, 512], F32)
        ar2.allocate()
        for nm, dst, key in (("sg", sgb, ("sgb",)), ("su", sub, ("sub",)),
                             ("sd", sdb, ("sdb",))):
            src = io[nm].rearrange("(cb p) i -> p cb i", p=128)
            dst3 = dst.rearrange("p (cb i) -> p cb i", i=1024)
            self.dma("gpsimd", dst3[:], src, writes=[key])
        for uh in range(2):
            for sb in range(8):
                for c in range(2):
                    uc = uh * 1024 + c * 512
                    pg, pu = self.pb[c % 2], self.pb[2 + c % 2]
                    for key, buf, pbi in ((("sgb",), sgb, c % 2),
                                          (("sub",), sub, 2 + c % 2)):
                        pbk = self.pb[pbi]
                        for cb in range(8):
                            def mmg(e, sb=sb, cb=cb, uc=uc, pbk=pbk, buf=buf):
                                return e.matmul(
                                    pbk[:], buf[:, cb * 1024 + sb * 128:
                                                cb * 1024 + sb * 128 + 128],
                                    self.xnl[:, cb * TLOC + uc:
                                             cb * TLOC + uc + 512],
                                    start=(cb == 0), stop=(cb == 7))

                            s.emit("tensor", mmg, reads=[key],
                                   writes=[("pb", pbi)] if cb == 0 else (),
                                   writes_accum=[("pb", pbi)] if cb > 0 else ())
                    s.emit("scalar", lambda e, pg=pg: e.activation(
                        scr[:], pg[:], AF.Silu),
                        reads=[("pb", c % 2)], writes=[("scr4",)])
                    s.emit("vector", lambda e, sb=sb, c=c, pu=pu: e.tensor_tensor(
                        hsh[:, sb * 1024 + c * 512:sb * 1024 + c * 512 + 512],
                        scr[:], pu[:], ALU.mult),
                        reads=[("scr4",), ("pb", 2 + c % 2)], writes=[("hsh", sb)])
            for fb in range(8):
                for c in range(2):
                    pbi = 4 + (fb * 2 + c) % 4
                    pbk = self.pb[pbi]
                    for sb in range(8):
                        def mmd(e, fb=fb, c=c, sb=sb, pbk=pbk):
                            return e.matmul(
                                pbk[:], sdb[:, sb * 1024 + fb * 128:
                                            sb * 1024 + fb * 128 + 128],
                                hsh[:, sb * 1024 + c * 512:sb * 1024 + c * 512 + 512],
                                start=(sb == 0), stop=(sb == 7))

                        s.emit("tensor", mmd, reads=[("sdb",), ("hsh", sb)],
                               writes=[("pb", pbi)] if sb == 0 else (),
                               writes_accum=[("pb", pbi)] if sb > 0 else ())
                    uc = uh * 1024 + c * 512

                    def dadd(e, fb=fb, uc=uc, pbk=pbk):
                        return e.tensor_tensor(
                            self.acc[:, fb * TLOC + uc:fb * TLOC + uc + 512],
                            self.acc[:, fb * TLOC + uc:fb * TLOC + uc + 512],
                            pbk[:], ALU.add)

                    s.emit("vector", dadd, reads=[("pb", pbi), ("acc", fb)],
                           writes=[("acc", fb)])
        if "acc5" in self.dbg:
            t = self.dbg_out("acc5", [128, 8 * TLOC])
            self.dma("sync", t[:], self.acc[:],
                     reads=[("acc", fb) for fb in range(8)])
        self.s.phase()
        ar2.free()

    def _p6_final(self):
        nc, s, io = self.nc, self.s, self.io
        ar = nc.auto_arena()
        xb = [ar.sbuf_tensor(f"xb{i}", [128, D], F32) for i in range(3)]
        ob = [ar.sbuf_tensor(f"ob{i}", [128, D], F32) for i in range(2)]
        ar.allocate()
        for t0 in range(2):
            self.dma("sync", xb[t0 % 3][:], io["xrot"][t0 * 128:(t0 + 1) * 128, :],
                     writes=[("xb", t0 % 3)])
        for tt in range(16):
            par = tt % 3
            if tt + 2 < 16:
                self.dma("sync", xb[(tt + 2) % 3][:],
                         io["xrot"][(tt + 2) * 128:(tt + 3) * 128, :],
                         writes=[("xb", (tt + 2) % 3)])
            for fb in range(8):
                pbi = (fb + tt * 8) % 4
                pbk = self.pb[pbi]

                def tr(e, tt=tt, fb=fb, pbk=pbk):
                    return e.transpose(
                        pbk[:, 0:128],
                        self.acc[:, fb * TLOC + tt * 128:fb * TLOC + tt * 128 + 128],
                        self.identf[:])

                s.emit("tensor", tr, reads=[("acc", fb), ("identf",)],
                       writes=[("pb", pbi)])

                opar = tt % 2

                def ev(e, tt=tt, fb=fb, par=par, opar=opar, pbk=pbk):
                    return e.tensor_tensor(ob[opar][:, fb * 128:(fb + 1) * 128],
                                           pbk[:, 0:128],
                                           xb[par][:, fb * 128:(fb + 1) * 128],
                                           ALU.add)

                s.emit("vector", ev, reads=[("pb", pbi), ("xb", par)],
                       writes=[("ob", opar, fb)])
            opar = tt % 2
            self.dma("sync", io["out"][tt * 128:(tt + 1) * 128, :], ob[opar][:],
                     reads=[("ob", opar, fb) for fb in range(8)])
        self.s.phase()
        ar.free()


def build_program(dbg=(), upto="full", pairs=range(8)):
    mk = MK(dbg=dbg)
    nc = mk.build(upto=upto, pairs=pairs)
    return nc, mk


def make_in_maps(inputs):
    x = np.ascontiguousarray(np.asarray(inputs["L0_attn_x"], dtype=np.float32))
    rel_bias = np.asarray(inputs["L0_attn_rel_bias"], dtype=np.float32)
    patches, ctab = host_tables(rel_bias)
    shared = {
        "Wq": inputs["L0_attn_Wq"], "Wk": inputs["L0_attn_Wk"],
        "Wv": inputs["L0_attn_Wv"], "Wo": inputs["L0_attn_Wo"],
        "bq": inputs["L0_attn_bq"], "bk": inputs["L0_attn_bk"],
        "bv": inputs["L0_attn_bv"], "bo": inputs["L0_attn_bo"],
        "gate_w": inputs["L0_ffn_gate_w"],
        "eg": np.asarray(inputs["L0_ffn_expert_gate"], dtype=np.float32)
            .reshape(NE, 8, 128, 4, 128).transpose(0, 3, 2, 1, 4),
        "eu": np.asarray(inputs["L0_ffn_expert_up"], dtype=np.float32)
            .reshape(NE, 8, 128, 4, 128).transpose(0, 3, 2, 1, 4),
        "ed": inputs["L0_ffn_expert_down"],
        "sg": inputs["L0_ffn_shared_gate"], "su": inputs["L0_ffn_shared_up"],
        "sd": inputs["L0_ffn_shared_down"],
    }
    shared = {k: np.ascontiguousarray(np.asarray(v, dtype=np.float32))
              for k, v in shared.items()}
    in_maps = []
    for core in range(8):
        b, half = core // 2, core % 2
        qbase = half * TLOC
        xr = np.ascontiguousarray(
            np.concatenate([x[b, qbase:], x[b, :qbase]], axis=0))
        m = dict(shared)
        m["xrot"] = xr
        m["patches"] = np.ascontiguousarray(patches[half])
        m["ctab"] = np.ascontiguousarray(ctab[half])
        in_maps.append(m)
    return in_maps


def kernel(**inputs):
    nc, _ = build_program()
    in_maps = make_in_maps(inputs)
    res = run_bass_kernel_spmd(nc, in_maps, core_ids=list(range(8)))
    out = np.empty((B, T, D), dtype=np.float32)
    for core in range(8):
        b, half = core // 2, core % 2
        qbase = half * TLOC
        out[b, qbase:qbase + TLOC] = res.results[core]["out"]
    return out


# version 38
# speedup vs baseline: 1.0352x; 1.0352x over previous
import math

import numpy as np

import concourse.bass as bass
import concourse.mybir as mybir
from concourse.bass_utils import run_bass_kernel_spmd

F32 = mybir.dt.float32
BF16 = mybir.dt.bfloat16
AF = mybir.ActivationFunctionType
ALU = mybir.AluOpType
AX = mybir.AxisListType
I32 = mybir.dt.int32
import struct as _struct
MAGICF = _struct.unpack('f', _struct.pack('I', 0x5f3759df))[0]

B, T, D = 4, 4096, 1024
NH, HD = 16, 64
TLOC = 2048
NE, EI = 16, 512
SI = 1024
TOPK = 4
SHIFT = 8.0
MAXREL = 16
NKB = T // 128
NPATCH = 17
EPS = 1e-5

ENGINES = ("tensor", "vector", "scalar", "gpsimd", "sync")
DMA_QUEUES = ("sync", "scalar", "gpsimd")


PQW = 1664


def pq_layout():
    out = []
    for uq in range(4):
        lo, hi = uq * 512, uq * 512 + 512
        ents = []
        off = 0
        for kb in range(NPATCH):
            w0, w1 = max(0, kb * 128 - 128), min(TLOC, kb * 128 + 256)
            c0, c1 = max(w0, lo), min(w1, hi)
            if c0 < c1:
                ents.append((kb, c0, c1, off))
                off += c1 - c0
        assert off <= PQW, off
        out.append(ents)
    return out


PQL = pq_layout()


def host_tables(rel_bias):
    rb = np.asarray(rel_bias)
    patches = np.zeros((2, NH, 4, 128, PQW), dtype=np.float32)
    ctab = np.zeros((2, NH, NKB, 2), dtype=np.float32)
    kp = np.arange(128)
    for half in range(2):
        for uq in range(4):
            for (kb, c0, c1, off) in PQL[uq]:
                u = np.arange(c0, c1)
                kprime = kb * 128 + kp
                dist = kprime[:, None] - u[None, :]
                if half == 1 and kb == 16:
                    dist = dist - T
                idx = np.clip(dist, -MAXREL + 1, MAXREL - 1) + MAXREL - 1
                patches[half, :, uq, :, off:off + (c1 - c0)] = np.transpose(
                    rb[idx], (2, 0, 1))
        for kb in range(NKB):
            if kb <= 15 or half == 0:
                ctab[half, :, kb, 0] = rb[2 * MAXREL - 2]
            else:
                ctab[half, :, kb, 0] = rb[0]
            ctab[half, :, kb, 1] = rb[0]
    return patches, ctab


class Sched:

    def __init__(self, nc):
        self.nc = nc
        self.phases = []
        self.cur = {e: [] for e in ENGINES}
        self.eng_idx = {e: 0 for e in ENGINES}
        self.dma_cnt = {q: 0 for q in DMA_QUEUES}
        self.last_w = {}
        self.readers = {}
        self.clock = {e: {} for e in ENGINES}
        self.flag = {e: set() for e in ENGINES}
        self.op_clock = {}
        self.esem = {e: nc.alloc_semaphore(f"es_{e}") for e in ENGINES}
        self.NDSEM = 12
        self.dsem = {(q, i): nc.alloc_semaphore(f"ds_{q}{i}")
                     for q in DMA_QUEUES for i in range(self.NDSEM)}
        self.dsem_total = {k: 0 for k in self.dsem}
        self.dma_slot = {q: 0 for q in DMA_QUEUES}

    def _dep_tokens(self, reads, writes, writes_accum):
        need = []
        for k in reads:
            t = self.last_w.get(k)
            if t is not None:
                need.append(t)
        for k in writes:
            t = self.last_w.get(k)
            if t is not None:
                need.append(t)
            need.extend(self.readers.get(k, ()))
        for k in writes_accum:
            need.extend(self.readers.get(k, ()))
        return need

    def emit(self, eng, fn, reads=(), writes=(), writes_accum=(), dma=False):
        assert eng in ENGINES
        need = self._dep_tokens(reads, writes, writes_accum)
        if dma:
            q = eng
            slot = (q, self.dma_slot[q])
            self.dma_slot[q] = (self.dma_slot[q] + 1) % self.NDSEM
            if self.dsem_total[slot] > 0:
                need = list(need) + [("D", slot, self.dsem_total[slot])]
            self.dsem_total[slot] += 16
            tok = ("D", slot, self.dsem_total[slot])
        else:
            tok = ("E", eng, self.eng_idx[eng])
        waits = []
        cl = self.clock[eng]
        for t in need:
            kind, st_, v = t
            if cl.get((kind, st_), -1) >= v:
                continue
            waits.append(t)
            cl[(kind, st_)] = max(cl.get((kind, st_), -1), v)
            for stx, sv in self.op_clock.get(t, {}).items():
                if cl.get(stx, -1) < sv:
                    cl[stx] = sv
            if kind == "E":
                self.flag[st_].add(v)
        op = dict(eng=eng, fn=fn, waits=waits, tok=tok, dma=dma)
        self.cur[eng].append(op)
        if dma:
            self.dma_cnt[eng] += 16
            op["dsem_slot"] = tok[1]
        else:
            self.eng_idx[eng] += 1
        snap = dict(cl)
        snap[(tok[0], tok[1])] = tok[2]
        self.op_clock[tok] = snap
        for k in reads:
            self.readers.setdefault(k, []).append(tok)
        for k in writes:
            self.last_w[k] = tok
            self.readers[k] = []
        for k in writes_accum:
            self.last_w[k] = tok
        return tok

    def phase(self):
        for q in DMA_QUEUES:
            waits = []
            cl = self.clock[q]
            for slot, tot in self.dsem_total.items():
                if slot[0] == q and tot > 0 and cl.get(("D", slot), -1) < tot:
                    waits.append(("D", slot, tot))
                    cl[("D", slot)] = tot
            if waits:
                self.cur[q].append(dict(eng=q, fn=None, waits=waits, tok=None,
                                        dma=False))
        self.phases.append(self.cur)
        self.cur = {e: [] for e in ENGINES}
        vis = {("E", e): self.eng_idx[e] - 1 for e in ENGINES}
        for slot, tot in self.dsem_total.items():
            if tot > 0:
                vis[("D", slot)] = tot
        for e in ENGINES:
            cl = self.clock[e]
            for st, sv in vis.items():
                if cl.get(st, -1) < sv:
                    cl[st] = sv

    def finish(self):
        if any(self.cur[e] for e in ENGINES):
            self.phase()
        fin = {e: [] for e in ENGINES}
        for slot, tot in self.dsem_total.items():
            if tot > 0:
                fin[slot[0]].append(("D", slot, tot))
        self.phases.append(
            {e: [dict(eng=e, fn=None, waits=fin[e], tok=None, dma=False)]
             if fin[e] else [] for e in ENGINES})
        self._emit()

    def _emit(self):
        nc = self.nc
        pref = {e: sorted(self.flag[e]) for e in ENGINES}

        def semval(e, idx):
            import bisect
            return bisect.bisect_right(pref[e], idx)

        for ph in self.phases:
            if not any(ph[e] for e in ENGINES):
                continue
            with nc.Block() as blk:
                for e in ENGINES:
                    ops = ph[e]

                    def mk(section_eng, section_ops):
                        def body(engobj):
                            for op in section_ops:
                                for kind, st_, v in op["waits"]:
                                    if kind == "E":
                                        engobj.wait_ge(self.esem[st_],
                                                       semval(st_, v))
                                    else:
                                        engobj.wait_ge(self.dsem[st_], v)
                                if op["fn"] is None:
                                    continue
                                instr = op["fn"](engobj)
                                if op["dma"]:
                                    instr.then_inc(self.dsem[op["dsem_slot"]], 16)
                                elif op["tok"][2] in self.flag[op["eng"]]:
                                    instr.then_inc(self.esem[op["eng"]], 1)
                        return body

                    getattr(blk, e)(mk(e, ops))


class MK:
    def __init__(self, dbg=()):
        self.nc = bass.Bass("TRN2", target_bir_lowering=False)
        self.s = Sched(self.nc)
        self.dbg = set(dbg)
        self.dbg_outs = {}
        self._declare_io()

    def _declare_io(self):
        nc = self.nc
        d = {}
        d["xrot"] = nc.dram_tensor("xrot", [T, D], F32, kind="ExternalInput")
        for w in ("Wq", "Wk", "Wv", "Wo"):
            d[w] = nc.dram_tensor(w, [D, D], F32, kind="ExternalInput")
        for v in ("bq", "bk", "bv", "bo"):
            d[v] = nc.dram_tensor(v, [D], F32, kind="ExternalInput")
        d["gate_w"] = nc.dram_tensor("gate_w", [NE, D], F32, kind="ExternalInput")
        d["eg"] = nc.dram_tensor("eg", [NE, 4, 128, 8, 128], F32,
                                 kind="ExternalInput")
        d["eu"] = nc.dram_tensor("eu", [NE, 4, 128, 8, 128], F32,
                                 kind="ExternalInput")
        d["ed"] = nc.dram_tensor("ed", [NE, EI, D], F32, kind="ExternalInput")
        d["sge"] = nc.dram_tensor("sge", [2, 4, 128, 8, 128], F32,
                                  kind="ExternalInput")
        d["sue"] = nc.dram_tensor("sue", [2, 4, 128, 8, 128], F32,
                                  kind="ExternalInput")
        d["sde"] = nc.dram_tensor("sde", [2, EI, D], F32, kind="ExternalInput")
        d["patches"] = nc.dram_tensor("patches", [NH, 4, 128, PQW], F32,
                                      kind="ExternalInput")
        d["ctab"] = nc.dram_tensor("ctab", [NH, NKB, 2], F32, kind="ExternalInput")
        d["out"] = nc.dram_tensor("out", [TLOC, D], F32, kind="ExternalOutput")
        d["xnrd"] = nc.dram_tensor("xnrd", [128, 8 * TLOC], BF16)
        d["ahd"] = nc.dram_tensor("ahd", [128, 8 * TLOC], BF16)
        d["combd"] = nc.dram_tensor("combd", [16, TLOC], F32)
        d["sumsd"] = nc.dram_tensor("sumsd", [16, TLOC], F32)
        d["recd"] = nc.dram_tensor("recd", [16, TLOC], F32)
        self.io = d

    def dbg_out(self, name, shape, dtype=F32):
        t = self.nc.dram_tensor("dbg_" + name, shape, dtype, kind="ExternalOutput")
        self.dbg_outs[name] = t
        return t

    def sb(self, name, shape, dtype):
        return self.nc.alloc_sbuf_tensor(name, shape, dtype)

    def dma(self, q, dst, src, reads=(), writes=(), **kw):
        return self.s.emit(q, lambda e: e.dma_start(dst, src, **kw), reads=reads,
                           writes=writes, dma=True)

    def build(self, upto="full", pairs=range(8)):
        self._alloc_persistent()
        self._p0_startup()
        if upto != "p0":
            self._p2_merged(pairs=pairs)
        if upto not in ("p0", "p2"):
            self._p3_oproj_shared()
            self._p6_final()
        self.s.finish()
        self.nc.finalize()
        return self.nc

    def _alloc_persistent(self):
        nc = self.nc
        self.pbig = nc.alloc_psum_tensor("pbig", [128, 4096], F32)
        self.pbig_bf = self.pbig.bitcast(BF16)
        self.pb = [self.pbig[:, i * 512:(i + 1) * 512] for i in range(8)]
        self.pb_bf = [self.pbig_bf[:, i * 1024:(i + 1) * 1024] for i in range(8)]

        self.xnl = self.sb("xnl", [128, 8 * TLOC], BF16)
        self.identf = self.sb("identf", [128, 128], F32)
        self.identb = self.sb("identb", [128, 128], BF16)
        self.m8 = self.sb("m8", [128, 1], F32)
        self.half = self.sb("half", [128, 1], F32)
        self.onesb = self.sb("onesb", [65, 128], F32)
        self.onesbf = self.sb("onesbf", [65, 128], BF16)
        self.b8 = {}
        for v in ("bq", "bk", "bv", "bo"):
            self.b8[v] = self.sb(v + "8", [128, 8], F32)
        self.ctabB = self.sb("ctabB", [128, NH * NKB * 2], F32)
        self.wqtp = self.sb("wqtp", [128, 8 * 128], BF16)
        self.wktp = self.sb("wktp", [128, 8 * 128], BF16)
        self.wvtp = self.sb("wvtp", [128, 8 * 128], BF16)
        self.wstage = self.sb("wstage", [128, 1024], BF16)

        self.ar_p0 = nc.auto_arena()
        a = self.ar_p0
        self.xt = [a.sbuf_tensor(f"xt{i}", [128, D], F32) for i in range(6)]
        self.sqs = a.sbuf_tensor("sqs", [128, D], F32)
        self.lnb = [a.sbuf_tensor(f"lnb{i}", [128, D], BF16) for i in range(4)]
        self.lnf = [a.sbuf_tensor(f"lnf{i}", [128, D], F32) for i in range(4)]
        self.xtf = a.sbuf_tensor("xtf", [128, 8 * 512], F32)
        self.xrstage = a.sbuf_tensor("xrstage", [128, 8 * 512], BF16)
        self.stat = [a.sbuf_tensor(f"stat{i}", [128, 8], F32) for i in range(4)]
        self.vb = [a.sbuf_tensor(f"vb{i}", [128, 4], F32) for i in range(2)]
        self.nb = [a.sbuf_tensor(f"nb{i}", [128, 4], F32) for i in range(2)]
        self.rb = [a.sbuf_tensor(f"rb{i}", [128, 4], F32) for i in range(2)]
        self.nmsb = [a.sbuf_tensor(f"nmsb{i}", [128, 4], F32) for i in range(2)]
        self.cmag = a.sbuf_tensor("cmag", [128, 1], F32)
        self.rtr = [a.sbuf_tensor(f"rtr_{i}", [128, 16], F32) for i in range(4)]
        self.rtr2 = [a.sbuf_tensor(f"rtr2_{i}", [128, 16], F32) for i in range(4)]
        self.rmax = [a.sbuf_tensor(f"rmax_{i}", [128, 1], F32) for i in range(4)]
        self.rden = [a.sbuf_tensor(f"rden_{i}", [128, 1], F32) for i in range(4)]
        self.comb = [a.sbuf_tensor(f"comb_{i}", [128, 16], F32) for i in range(4)]
        self.comball = a.sbuf_tensor("comball", [128, 256], F32)
        self.ct_row = a.sbuf_tensor("ct_row", [1, NH * NKB * 2], F32)
        self.gw_sb = a.sbuf_tensor("gw_sb", [16, D], F32)
        self.gwT = a.sbuf_tensor("gwT", [128, 8 * 16], F32)
        self.combT = a.sbuf_tensor("combT", [16, TLOC], F32)

        self.ar_c = nc.auto_arena()
        self.acc = self.ar_c.sbuf_tensor("acc", [128, 8 * TLOC], F32)

        self.ar_m = nc.auto_arena()
        a = self.ar_m
        self.qp = a.sbuf_tensor("qp", [128, TLOC], BF16)
        self.kt = a.sbuf_tensor("kt", [128, T], BF16)
        self.vt = a.sbuf_tensor("vt", [128, TLOC], BF16)
        self.va = a.sbuf_tensor("va", [128, NKB * 130], BF16)
        self.pq = [[a.sbuf_tensor(f"pq{h}_{i}", [128, PQW], BF16)
                    for i in range(2)] for h in range(2)]
        self.pt = [a.sbuf_tensor(f"pt{i}", [128, 1024], BF16) for i in range(3)]
        self.srow = a.sbuf_tensor("srow", [1, 512], F32)
        self.srq = [a.sbuf_tensor(f"srq{h}", [128, 4], F32) for h in range(2)]
        self.rbc2 = a.sbuf_tensor("rbc2", [64, 512], F32)

        self.xrbuf = [a.sbuf_tensor(f"xrbuf{i}", [128, 8 * 256], BF16)
                      for i in range(2)]
        self.ahstg = [a.sbuf_tensor(f"ahstg{h}", [64, 512], BF16)
                      for h in range(2)]
        self.wgs = [a.sbuf_tensor(f"wgs{i}", [128, 8 * 128], BF16) for i in range(2)]
        self.wus = [a.sbuf_tensor(f"wus{i}", [128, 8 * 128], BF16) for i in range(2)]
        self.wds = [a.sbuf_tensor(f"wds{i}", [128, 4 * 1024], BF16)
                    for i in range(2)]
        self.he = a.sbuf_tensor("he", [128, 4 * 1024], BF16)
        self.ce = a.sbuf_tensor("ce", [128, TLOC], BF16)
        self.scr5 = [a.sbuf_tensor(f"scr5_{i}", [128, 512], BF16) for i in range(2)]
        self.scr5b = [a.sbuf_tensor(f"scr5b_{i}", [128, 512], BF16)
                      for i in range(2)]

        self.ar_p0.allocate(side="right")

    def _p0_startup(self):
        nc, s = self.nc, self.s
        io = self.io

        s.emit("gpsimd", lambda e: e.iota(self.identf[:], [[1, 128]], base=0,
                                          channel_multiplier=-1,
                                          allow_small_or_imprecise_dtypes=True),
               writes=[("identf",)])
        s.emit("vector", lambda e: e.tensor_scalar(self.identf[:], self.identf[:],
                                                   0.0, None, ALU.is_equal),
               reads=[("identf",)], writes=[("identf",)])
        s.emit("vector", lambda e: e.tensor_copy(self.identb[:], self.identf[:]),
               reads=[("identf",)], writes=[("identb",)])
        s.emit("vector", lambda e: e.memset(self.m8[:], -SHIFT), writes=[("m8",)])
        s.emit("vector", lambda e: e.memset(self.half[:], 0.5), writes=[("half",)])
        s.emit("vector", lambda e: e.memset(self.cmag[:], MAGICF),
               writes=[("cmag",)])
        self.ar_c.allocate()
        for fb in range(8):
            s.emit("vector", lambda e, fb=fb: e.memset(
                self.acc[:, fb * TLOC:(fb + 1) * TLOC], 0.0),
                writes=[("acc", fb)])
        s.emit("vector", lambda e: e.memset(self.onesb[0:1, :], 1.0),
               writes=[("ones",)])
        s.emit("vector", lambda e: e.memset(self.onesb[64:65, :], 1.0),
               writes=[("ones64",)])
        s.emit("vector", lambda e: e.memset(self.onesbf[0:1, :], 1.0),
               writes=[("onesbf",)])
        s.emit("vector", lambda e: e.memset(self.onesbf[64:65, :], 1.0),
               writes=[("onesbf64",)])
        for v in ("bq", "bk", "bv", "bo"):
            dst = self.b8[v]
            src = io[v].rearrange("(a p) -> p a", p=128)
            self.dma("sync", dst[:], src, writes=[("b8", v)],
                     allow_slow_non_contiguous=True)

        ct_row = self.ct_row
        self.dma("sync", ct_row[:], io["ctab"].rearrange("h k r -> (h k r)")[None, :],
                 writes=[("ct_row",)])
        ncol = NH * NKB * 2
        for c0 in range(0, ncol, 512):
            w = min(512, ncol - c0)
            pbk = self.pb[c0 // 512 % 2]

            def mm(e, c0=c0, w=w, pbk=pbk):
                return e.matmul(pbk[:, 0:w], self.onesb[0:1, 0:128],
                                ct_row[:, c0:c0 + w], start=True, stop=True)

            s.emit("tensor", mm, reads=[("ones",), ("ct_row",)],
                   writes=[("pb", c0 // 512 % 2)])
            s.emit("vector", lambda e, c0=c0, w=w, pbk=pbk: e.tensor_scalar(
                self.ctabB[:, c0:c0 + w], pbk[:, 0:w], -SHIFT, None, ALU.add),
                reads=[("pb", c0 // 512 % 2)], writes=[("ctabB",)])

        gw_sb = self.gw_sb
        self.dma("sync", gw_sb[:], io["gate_w"][:], writes=[("gw_sb",)])
        for cb in range(8):
            pbk = self.pb[2 + cb % 2]

            def tr(e, cb=cb, pbk=pbk):
                return e.transpose(pbk[:, 0:16], gw_sb[:, cb * 128:(cb + 1) * 128],
                                   self.identf[0:16, 0:16])

            s.emit("tensor", tr, reads=[("gw_sb",), ("identf",)],
                   writes=[("pb", 2 + cb % 2)])
            s.emit("vector", lambda e, cb=cb, pbk=pbk: e.tensor_copy(
                self.gwT[:, cb * 16:(cb + 1) * 16], pbk[:, 0:16]),
                reads=[("pb", 2 + cb % 2)], writes=[("gwT",)])

        xt, lnb, lnf = self.xt, self.lnb, self.lnf
        xtf = self.xtf

        def ln_a(tl):
            ti = tl % 6
            x = xt[ti]
            st = self.stat[tl % 4]
            sqs = self.sqs
            sk, qk = ("st", tl % 4), ("sqs",)
            s.emit("scalar", lambda e: e.activation(
                sqs[:], x[:], AF.Square, accum_out=st[:, 2:3]),
                reads=[("xt", ti)], writes=[qk, (sk, "sq")])
            s.emit("scalar", lambda e: e.activation(
                sqs[:], x[:], AF.Identity, accum_out=st[:, 0:1]),
                reads=[("xt", ti), qk], writes=[qk, (sk, "sum")])

        def ln_v1(tl):
            st = self.stat[tl % 4]
            sk = ("st", tl % 4)
            g2 = (tl // 4) % 2
            c = tl % 4
            nbc = self.nb[g2][:, c:c + 1]
            vbc = self.vb[g2][:, c:c + 1]
            s.emit("vector", lambda e: e.tensor_scalar(nbc, st[:, 0:1],
                                                       -1.0 / D, None, ALU.mult),
                   reads=[(sk, "sum")], writes=[("nb", g2)])
            s.emit("vector", lambda e: e.tensor_tensor(st[:, 5:6], nbc,
                                                       nbc, ALU.mult),
                   reads=[("nb", g2)], writes=[(sk, "m2")])
            s.emit("vector", lambda e: e.scalar_tensor_tensor(
                vbc, st[:, 2:3], 1.0 / D, st[:, 5:6], ALU.mult, ALU.subtract),
                reads=[(sk, "sq"), (sk, "m2")], writes=[("vb", g2)])
            s.emit("vector", lambda e: e.tensor_scalar(vbc, vbc, EPS,
                                                       None, ALU.add),
                   reads=[("vb", g2)], writes=[("vb", g2)])

        def rsqrt_group(g):
            g2 = g % 2
            vbv, rbv = self.vb[g2], self.rb[g2]
            tmp = self.nmsb[g2]
            vbi, rbi = vbv.bitcast(I32), rbv.bitcast(I32)
            cmagi = self.cmag.bitcast(I32)
            s.emit("vector", lambda e: e.tensor_scalar(
                rbi[:], vbi[:], 1, None, ALU.arith_shift_right),
                reads=[("vb", g2)], writes=[("rb", g2)])
            s.emit("vector", lambda e: e.tensor_tensor(
                rbi[:], cmagi[:].to_broadcast([128, 4]), rbi[:], ALU.subtract),
                reads=[("cmag",), ("rb", g2)], writes=[("rb", g2)])
            for _ in range(3):
                s.emit("vector", lambda e: e.tensor_tensor(tmp[:], rbv[:], rbv[:],
                                                           ALU.mult),
                       reads=[("rb", g2)], writes=[("nmsb", g2)])
                s.emit("vector", lambda e: e.tensor_tensor(tmp[:], tmp[:], vbv[:],
                                                           ALU.mult),
                       reads=[("nmsb", g2), ("vb", g2)], writes=[("nmsb", g2)])
                s.emit("vector", lambda e: e.tensor_scalar(tmp[:], tmp[:], -0.5,
                                                           1.5, ALU.mult, ALU.add),
                       reads=[("nmsb", g2)], writes=[("nmsb", g2)])
                s.emit("vector", lambda e: e.tensor_tensor(rbv[:], rbv[:], tmp[:],
                                                           ALU.mult),
                       reads=[("rb", g2), ("nmsb", g2)], writes=[("rb", g2)])
            s.emit("vector", lambda e: e.tensor_tensor(tmp[:], self.nb[g2][:],
                                                       rbv[:], ALU.mult),
                   reads=[("nb", g2), ("rb", g2)], writes=[("nmsb", g2)])

        def ln_s2(tl):
            ti = tl % 6
            local = tl < 16
            dst = lnf[tl % 4] if local else lnb[tl % 4]
            key = ("lnf", tl % 4) if local else ("lnb", tl % 4)
            g2 = (tl // 4) % 2
            c = tl % 4
            s.emit("scalar", lambda e, dst=dst: e.activation(
                dst[:], xt[ti][:], AF.Identity,
                bias=self.nmsb[g2][:, c:c + 1], scale=self.rb[g2][:, c:c + 1]),
                reads=[("xt", ti), ("nmsb", g2), ("rb", g2)], writes=[key])

        def emit_dma(tl):
            ti = tl % 6
            rows = tl * 128
            self.dma("sync", xt[ti][:], io["xrot"][rows:rows + 128, :],
                     writes=[("xt", ti)])

        def post_tile(tl):
            ti = tl % 4
            local = tl < 16
            if local:
                for cb in range(8):
                    pbi = (tl * 8 + cb) % 2
                    pbk = self.pb[pbi]

                    def tr(e, cb=cb, ti=ti, pbk=pbk):
                        return e.transpose(pbk[:, 0:128],
                                           lnf[ti][:, cb * 128:(cb + 1) * 128],
                                           self.identf[:])

                    s.emit("tensor", tr, reads=[("lnf", ti), ("identf",)],
                           writes=[("pb", pbi)])
                    s.emit("scalar", lambda e, cb=cb, ti=ti, pbk=pbk:
                           e.activation(xtf[:, cb * 512 + ti * 128:
                                            cb * 512 + ti * 128 + 128],
                                        pbk[:, 0:128], AF.Identity),
                           reads=[("pb", pbi)], writes=[("xtf", cb)])
                rp = tl % 4
                pbi = 2 + tl % 2
                pbk = self.pb[pbi]
                for cb in range(8):
                    def mm(e, cb=cb, ti=ti, pbk=pbk):
                        return e.matmul(pbk[:, 0:16],
                                        xtf[:, cb * 512 + ti * 128:
                                            cb * 512 + ti * 128 + 128],
                                        self.gwT[:, cb * 16:(cb + 1) * 16],
                                        start=(cb == 0), stop=(cb == 7))

                    s.emit("tensor", mm, reads=[("xtf", cb), ("gwT",)],
                           writes=[("pb", pbi)] if cb == 0 else (),
                           writes_accum=[("pb", pbi)] if cb > 0 else ())
                s.emit("scalar", lambda e, pbk=pbk, rp=rp: e.activation(
                    self.rtr[rp][:], pbk[:, 0:16], AF.Sigmoid),
                    reads=[("pb", pbi)], writes=[("rtr", rp)])
            else:
                for cb in range(8):
                    pbi = (tl * 8 + cb) % 2
                    pbk = self.pb_bf[pbi]

                    def tr(e, cb=cb, ti=ti, pbk=pbk):
                        return e.transpose(pbk[:, 0:128],
                                           lnb[ti][:, cb * 128:(cb + 1) * 128],
                                           self.identb[:])

                    s.emit("tensor", tr, reads=[("lnb", ti), ("identb",)],
                           writes=[("pb", pbi)])
                    s.emit("vector", lambda e, cb=cb, ti=ti, pbk=pbk:
                           e.tensor_copy(
                        self.xrstage[:, cb * 512 + ti * 128:
                                     cb * 512 + ti * 128 + 128],
                        pbk[:, 0:128]),
                        reads=[("pb", pbi)], writes=[("xrs", cb)])

        def routing_vec(tl):
            rp = tl % 4
            rtr, rtr2 = self.rtr[rp], self.rtr2[rp]
            rmax, rden, comb = self.rmax[rp], self.rden[rp], self.comb[rp]
            s.emit("vector", lambda e: e.tensor_copy(rtr2[:], rtr[:]),
                   reads=[("rtr", rp)], writes=[("rtr2", rp)])
            for it in range(TOPK):
                s.emit("vector", lambda e: e.reduce_max(rmax[:], rtr2[:],
                                                        axis=AX.X),
                       reads=[("rtr2", rp)], writes=[("rmax", rp)])
                if it < TOPK - 1:
                    s.emit("vector", lambda e: e.tensor_scalar(
                        comb[:], rtr2[:], rmax[:], None, ALU.is_ge),
                        reads=[("rtr2", rp), ("rmax", rp)], writes=[("comb", rp)])
                    s.emit("vector", lambda e: e.scalar_tensor_tensor(
                        rtr2[:], comb[:], -1000.0, rtr2[:], ALU.mult,
                        ALU.add),
                        reads=[("comb", rp), ("rtr2", rp)], writes=[("rtr2", rp)])
            s.emit("vector", lambda e: e.tensor_scalar(
                comb[:], rtr[:], rmax[:], None, ALU.is_ge),
                reads=[("rtr", rp), ("rmax", rp)], writes=[("comb", rp)])
            s.emit("vector", lambda e: e.tensor_tensor(comb[:], comb[:],
                                                       rtr[:], ALU.mult),
                   reads=[("comb", rp), ("rtr", rp)], writes=[("comb", rp)])
            s.emit("vector", lambda e: e.reduce_sum(rden[:], comb[:],
                                                    axis=AX.X),
                   reads=[("comb", rp)], writes=[("rden", rp)])
            s.emit("vector", lambda e: e.tensor_scalar(rden[:], rden[:],
                                                       1e-20, None, ALU.add),
                   reads=[("rden", rp)], writes=[("rden", rp)])
            s.emit("vector", lambda e: e.reciprocal(rden[:], rden[:]),
                   reads=[("rden", rp)], writes=[("rden", rp)])
            s.emit("vector", lambda e: e.tensor_scalar(
                self.comball[:, tl * 16:tl * 16 + 16], comb[:], rden[:],
                None, ALU.mult),
                reads=[("comb", rp), ("rden", rp)], writes=[("comball", tl)])

        def chunk_close(ch):
            tbase = (ch % 4) * 512
            if ch < 4:
                for cb in range(8):
                    s.emit("vector", lambda e, cb=cb, tbase=tbase: e.tensor_copy(
                        self.xnl[:, cb * TLOC + tbase:cb * TLOC + tbase + 512],
                        xtf[:, cb * 512:(cb + 1) * 512]),
                        reads=[("xtf", cb)], writes=[("xn", cb)])
            else:
                dst = io["xnrd"].rearrange("p (cb t) -> p cb t", t=TLOC)[
                    :, :, tbase:tbase + 512]
                src = self.xrstage.rearrange("p (cb t) -> p cb t", t=512)
                self.dma("sync", dst, src[:],
                         reads=[("xrs", cb) for cb in range(8)])

        NTL = 32
        emit_dma(0)
        emit_dma(1)
        for t in range(NTL + 6):
            if 4 <= t <= NTL + 3:
                ln_s2(t - 4)
                post_tile(t - 4)
                if (t - 4) % 4 == 3:
                    chunk_close((t - 4) // 4)
            if t < NTL:
                if t + 2 < NTL:
                    emit_dma(t + 2)
                ln_a(t)
                ln_v1(t)
                if t % 4 == 3:
                    rsqrt_group(t // 4)
            if 6 <= t <= NTL + 5 and t - 6 < 16:
                routing_vec(t - 6)
        self.emit_wdmas(0)
        for idx in range(16):
            pbt = self.pb[4 + idx % 2]
            s.emit("tensor", lambda e, idx=idx, pbt=pbt: e.transpose(
                pbt[0:16, 0:128], self.comball[:, idx * 16:idx * 16 + 16],
                self.identf[:]),
                reads=[("comball", idx), ("identf",)], writes=[("pb", 4 + idx % 2)])
            s.emit("vector", lambda e, idx=idx, pbt=pbt: e.tensor_copy(
                self.combT[:, idx * 128:idx * 128 + 128], pbt[0:16, 0:128]),
                reads=[("pb", 4 + idx % 2)], writes=[("combT",)])
        self.dma("sync", io["combd"][:], self.combT[:], reads=[("combT",)])
        if "xnl" in self.dbg:
            t = self.dbg_out("xnl", [128, 8 * TLOC], BF16)
            self.dma("sync", t[:], self.xnl[:],
                     reads=[("xn", cb) for cb in range(8)])
        if "combT" in self.dbg:
            t = self.dbg_out("combT", [16, TLOC])
            self.dma("sync", t[:], self.combT[:], reads=[("combT",)])
        self.s.phase()
        self.ar_p0.free()

    def emit_wdmas(self, hp):
        s, io = self.s, self.io
        for srcname, dst, key in (("Wq", self.wqtp, ("wqtp",)),
                                  ("Wk", self.wktp, ("wktp",)),
                                  ("Wv", self.wvtp, ("wvtp",))):
            src = io[srcname][hp * 128:(hp + 1) * 128, :]
            self.dma("gpsimd", self.wstage[:], src,
                     reads=[], writes=[("wstage",)])
            dst3 = dst.rearrange("p (cb f) -> p cb f", f=128)
            src3 = self.wstage.rearrange("p (cb f) -> p cb f", f=128)
            self.dma("sync", dst3[:], src3[:], reads=[("wstage",)],
                     writes=[key], transpose=True)

    def _pair_setup(self, hp, step):
        s, io = self.s, self.io
        for c in range(2):
            xb = self.xrbuf[c % 2]
            src = io["xnrd"].rearrange("p (cb t) -> p cb t", t=TLOC)[
                :, :, c * 256:(c + 1) * 256]
            self.dma("sync", xb.rearrange("p (cb t) -> p cb t", t=256)[:],
                     src, writes=[("xrb", c % 2, 0), ("xrb", c % 2, 1)])
        va3 = self.va.rearrange("p (kb c) -> p kb c", c=130)
        s.emit("vector", lambda e: e.memset(va3[:, :, 64:65], 1.0),
               writes=[("va", "ones")])
        s.emit("vector", lambda e: e.memset(va3[:, :, 129:130], 1.0),
               writes=[("va", "ones2")])

        def proj_local(dst, dkey, wt, wkey, bias, scale, c):
            pbi = c % 2
            pbk = self.pb[pbi]
            for cb in range(8):
                def mm(e, cb=cb, pbk=pbk, c=c, wt=wt):
                    return e.matmul(pbk[:], wt[:, cb * 128:(cb + 1) * 128],
                                    self.xnl[:, cb * TLOC + c * 512:
                                             cb * TLOC + c * 512 + 512],
                                    start=(cb == 0), stop=(cb == 7))

                s.emit("tensor", mm, reads=[wkey, ("xn", cb)],
                       writes=[("pb", pbi)] if cb == 0 else (),
                       writes_accum=[("pb", pbi)] if cb > 0 else ())
            if scale is None:
                s.emit("vector", lambda e, c=c, pbk=pbk, dst=dst: e.tensor_scalar(
                    dst[:, c * 512:(c + 1) * 512], pbk[:], bias, None, ALU.add),
                    reads=[("pb", pbi)], writes=[dkey])
            else:
                s.emit("vector", lambda e, c=c, pbk=pbk, dst=dst: e.tensor_scalar(
                    dst[:, c * 512:(c + 1) * 512], pbk[:], bias, scale,
                    ALU.add, ALU.mult),
                    reads=[("pb", pbi)], writes=[dkey])

        for c in range(4):
            proj_local(self.qp, ("qp",), self.wqtp, ("wqtp",),
                       self.b8["bq"][:, hp:hp + 1], 0.125, c)
        for c in range(4):
            proj_local(self.kt, ("kt",), self.wktp, ("wktp",),
                       self.b8["bk"][:, hp:hp + 1], None, c)
        for c in range(4):
            proj_local(self.vt, ("vt",), self.wvtp, ("wvtp",),
                       self.b8["bv"][:, hp:hp + 1], None, c)
        self._v_transpose(0)
        for c in range(8):
            par = c % 2
            xb = self.xrbuf[par]
            if c >= 2:
                src = io["xnrd"].rearrange("p (cb t) -> p cb t", t=TLOC)[
                    :, :, c * 256:(c + 1) * 256]
                dst = xb.rearrange("p (cb t) -> p cb t", t=256)
                self.dma("sync", dst[:, 0:4], src[:, 0:4],
                         writes=[("xrb", par, 0)])
                self.dma("sync", dst[:, 4:8], src[:, 4:8],
                         writes=[("xrb", par, 1)])
            for which in ("k", "v"):
                pbi = 2 + c % 2
                pbk = self.pb[pbi]
                wt, wkey = ((self.wktp, ("wktp",)) if which == "k"
                            else (self.wvtp, ("wvtp",)))
                bias = (self.b8["bk"] if which == "k" else
                        self.b8["bv"])[:, hp:hp + 1]
                for cb in range(8):
                    def mm(e, cb=cb, pbk=pbk, xb=xb, wt=wt):
                        return e.matmul(pbk[:, 0:256],
                                        wt[:, cb * 128:(cb + 1) * 128],
                                        xb[:, cb * 256:(cb + 1) * 256],
                                        start=(cb == 0), stop=(cb == 7))

                    s.emit("tensor", mm,
                           reads=[wkey, ("xrb", par, cb // 4)],
                           writes=[("pb", pbi)] if cb == 0 else (),
                           writes_accum=[("pb", pbi)] if cb > 0 else ())
                if which == "k":
                    col = 2048 + c * 256
                    wr = [("ktr",)]
                    dstt = self.kt
                else:
                    col = c * 256
                    wr = [("vt",), ("vtr",)]
                    dstt = self.vt

                def ev(e, col=col, pbk=pbk, dstt=dstt, bias=bias):
                    return e.tensor_scalar(dstt[:, col:col + 256], pbk[:, 0:256],
                                           bias, None, ALU.add)

                s.emit("vector", ev, reads=[("pb", pbi)], writes=wr)
        self._v_transpose(1)

    def _v_transpose(self, vh):
        s = self.s
        va3 = self.va.rearrange("p (kb c) -> p kb c", c=130)
        vkey = ("vt",) if vh == 0 else ("vtr",)
        for kk in range(16):
            kb = vh * 16 + kk
            pbi = 2 + kk % 2
            pbk = self.pb_bf[pbi]
            s.emit("tensor", lambda e, kk=kk, pbk=pbk: e.transpose(
                pbk[:, 0:128], self.vt[:, kk * 128:(kk + 1) * 128], self.identb[:]),
                reads=[vkey, ("identb",)], writes=[("pb", pbi)])
            s.emit("vector", lambda e, kb=kb, pbk=pbk, va3=va3: e.tensor_copy(
                va3[:, kb, 0:64], pbk[:, 0:64]),
                reads=[("pb", pbi)], writes=[("va", kb, 0)])
            s.emit("vector", lambda e, kb=kb, pbk=pbk, va3=va3: e.tensor_copy(
                va3[:, kb, 65:129], pbk[:, 64:128]),
                reads=[("pb", pbi)], writes=[("va", kb, 1)])

    def _expert_units(self, e):
        s, io = self.s, self.io
        par = e % 2
        wd = self.wds[par]
        rot = [0]

        def dma_wd():
            te = io["sde"][e - NE] if e >= NE else io["ed"][e]
            src = te.rearrange("(ib p) f -> p ib f", p=128)
            dst3 = wd.rearrange("p (ib f) -> p ib f", ib=4)
            self.dma("gpsimd", dst3[:], src, writes=[("wd", par)])

        def dma_gu(ib):
            spar = ib % 2
            for mat in ("eg", "eu"):
                if e >= NE:
                    src = io["sge" if mat == "eg" else "sue"][e - NE, ib]
                else:
                    src = io[mat][e, ib]
                buf = (self.wgs if mat == "eg" else self.wus)[spar]
                dst3 = buf.rearrange("p (cb i) -> p cb i", cb=8)
                self.dma("gpsimd", dst3[:], src, writes=[(mat, spar)])

        def build_ce():
            if e >= NE:
                s.emit("vector", lambda e_: e_.memset(self.ce[:], 0.5),
                       writes=[("ce",)])
                return
            src = bass.AP(io["combd"], e * TLOC, [[0, 128], [1, TLOC]])
            self.dma("gpsimd", self.ce[:], src, writes=[("ce",)])
            s.emit("vector", lambda e_: e_.tensor_scalar(
                self.ce[:], self.ce[:], 0.5, None, ALU.mult),
                reads=[("ce",)], writes=[("ce",)])

        def g_unit(ib, cc, uh, mat):
            spar = ib % 2
            pbi = 4 + 2 * (rot[0] % 2) + (0 if mat == "eg" else 1)
            uc = uh * 1024 + cc * 512
            buf = (self.wgs if mat == "eg" else self.wus)[spar]
            pbk = self.pb[pbi]
            for cb in range(8):
                def mmg(e_, cb=cb, buf=buf, uc=uc, pbk=pbk):
                    return e_.matmul(
                        pbk[:], buf[:, cb * 128:(cb + 1) * 128],
                        self.xnl[:, cb * TLOC + uc:cb * TLOC + uc + 512],
                        start=(cb == 0), stop=(cb == 7))

                s.emit("tensor", mmg, reads=[(mat, spar)],
                       writes=[("pb", pbi)] if cb == 0 else (),
                       writes_accum=[("pb", pbi)] if cb > 0 else ())
            return pbi

        def gu_epilogue(ib, cc, uh, bg, bu):
            pr = rot[0] % 2
            rot[0] += 1
            sc1, sc2 = self.scr5[pr], self.scr5b[pr]
            uc = uh * 1024 + cc * 512
            s.emit("scalar", lambda e_, bg=bg, sc1=sc1: e_.activation(
                sc1[:], self.pb[bg][:], AF.Tanh, scale=self.half[:]),
                reads=[("pb", bg), ("half",)], writes=[("scr5", pr)])
            s.emit("vector", lambda e_, bg=bg, sc1=sc1, sc2=sc2:
                   e_.scalar_tensor_tensor(
                sc2[:], sc1[:], 1.0, self.pb[bg][:], ALU.add, ALU.mult),
                reads=[("scr5", pr), ("pb", bg)], writes=[("scr5b", pr)])
            s.emit("vector", lambda e_, bu=bu, sc2=sc2: e_.tensor_tensor(
                sc2[:], sc2[:], self.pb[bu][:], ALU.mult),
                reads=[("scr5b", pr), ("pb", bu)], writes=[("scr5b", pr)])
            s.emit("vector", lambda e_, ib=ib, cc=cc, uc=uc, sc2=sc2:
                   e_.tensor_tensor(
                self.he[:, ib * 1024 + cc * 512:ib * 1024 + cc * 512 + 512],
                sc2[:], self.ce[:, uc:uc + 512], ALU.mult),
                reads=[("scr5b", pr), ("ce",)], writes=[("he", ib, cc)])

        def down_unit(fb, cc, uh):
            pbi = 4 + rot[0] % 4
            rot[0] += 1
            pbk = self.pb[pbi]
            for ib in range(4):
                def mmd(e_, fb=fb, cc=cc, ib=ib, pbk=pbk):
                    return e_.matmul(
                        pbk[:], wd[:, ib * 1024 + fb * 128:
                                   ib * 1024 + fb * 128 + 128],
                        self.he[:, ib * 1024 + cc * 512:
                                ib * 1024 + cc * 512 + 512],
                        start=(ib == 0), stop=(ib == 3))

                s.emit("tensor", mmd, reads=[("wd", par), ("he", ib, cc)],
                       writes=[("pb", pbi)] if ib == 0 else (),
                       writes_accum=[("pb", pbi)] if ib > 0 else ())
            uc = uh * 1024 + cc * 512

            def dadd(e_, fb=fb, uc=uc, pbk=pbk):
                return e_.tensor_tensor(
                    self.acc[:, fb * TLOC + uc:fb * TLOC + uc + 512],
                    self.acc[:, fb * TLOC + uc:fb * TLOC + uc + 512],
                    pbk[:], ALU.add)

            s.emit("vector", dadd, reads=[("pb", pbi), ("acc", fb)],
                   writes=[("acc", fb)])

        yield lambda: (dma_wd(), dma_gu(0))
        yield build_ce

        def gu_pair(ib, cc, uh, pre=None):
            def first():
                if pre is not None:
                    pre()
                g_unit(ib, cc, uh, "eg")

            def second():
                bu = g_unit(ib, cc, uh, "eu")
                gu_epilogue(ib, cc, uh, bu - 1, bu)
            return first, second

        for uh in range(2):
            for ib in range(4):
                nxt = ib + 1
                if nxt < 4:
                    pre = (lambda nxt=nxt: dma_gu(nxt))
                elif uh == 0:
                    pre = (lambda: dma_gu(0))
                else:
                    pre = None
                for cc in range(2):
                    f1, f2 = gu_pair(ib, cc, uh, pre if cc == 0 else None)
                    yield f1
                    yield f2
            for fb in range(8):
                for cc in range(2):
                    yield lambda fb=fb, cc=cc, uh=uh: down_unit(fb, cc, uh)

    def _load_pq(self, h, uq, slot):
        h01 = h % 2
        buf = self.pq[h01][slot]
        src = self.io["patches"][h, uq]
        self.dma("gpsimd", buf[:], src, writes=[("pq", h01, slot)])

    def _pair_heads(self, hp, step, next_hp):
        s, io = self.s, self.io
        h0, h1 = 2 * hp, 2 * hp + 1
        va3 = self.va.rearrange("p (kb c) -> p kb c", c=130)
        for uq in range(4):
            u0 = uq * 512
            slot = uq % 2
            if not self._pq0_done:
                self._load_pq(h0, 0, 0)
                self._load_pq(h1, 0, 0)
                self._pq0_done = True
            pql = PQL[uq]
            pqkb = {kb: (c0, c1, off) for (kb, c0, c1, off) in pql}

            def emit_av(kb):
                par = kb % 3
                ptt = self.pt[par]
                for h01 in range(2):
                    bank = 2 + h01

                    def mv(e, kb=kb, h01=h01, ptt=ptt):
                        pc = 1024 + h01 * 512
                        return e.matmul(self.pbig[0:65, pc:pc + 512],
                                        va3[:, kb, 65 * h01:65 * h01 + 65],
                                        ptt[:, h01 * 512:h01 * 512 + 512],
                                        start=(kb == 0), stop=(kb == NKB - 1))

                    s.emit("tensor", mv,
                           reads=[("va", kb, h01), ("va", "ones"), ("va", "ones2"),
                                  ("pt", par)],
                           writes=[("pb", 2 + h01)] if kb == 0 else (),
                           writes_accum=[("pb", 2 + h01)] if kb > 0 else ())

            for kb in range(NKB):
                if kb in (6, 12, 14, 20) and self._norm_q:
                    self._norm_q.popleft()()
                if kb == 20:
                    nuq = uq + 1
                    if nuq < 4:
                        self._load_pq(h0, nuq, (uq + 1) % 2)
                        self._load_pq(h1, nuq, (uq + 1) % 2)
                    elif next_hp is not None:
                        self._load_pq(2 * next_hp, 0, (uq + 1) % 2)
                        self._load_pq(2 * next_hp + 1, 0, (uq + 1) % 2)
                par = kb % 3
                ptt = self.pt[par]
                if kb <= NPATCH - 1:
                    w0g = kb * 128 - 128
                    w0, w1 = max(0, w0g), min(TLOC, kb * 128 + 256)
                else:
                    w0 = w1 = TLOC
                drained = getattr(step, "drained", False)
                for h01 in range(2):
                    h = 2 * hp + h01
                    rbase = 64 * h01
                    sbank = (6 + h01) if (drained and kb % 2) else h01
                    scol = sbank * 512

                    def mm(e, kb=kb, rbase=rbase, u0=u0, scol=scol, h01=h01):
                        return e.matmul(
                            self.pbig[:, scol:scol + 512],
                            self.kt[rbase:rbase + 64, kb * 128:(kb + 1) * 128],
                            self.qp[rbase:rbase + 64, u0:u0 + 512],
                            start=True, stop=True,
                            tile_position=(rbase, 0))

                    s.emit("tensor", mm, reads=[("kt",), ("ktr",), ("qp",)],
                           writes=[("pb", sbank)])
                    cta = self.ctabB[:, (h * NKB + kb) * 2:
                                     (h * NKB + kb) * 2 + 1]
                    ctb = self.ctabB[:, (h * NKB + kb) * 2 + 1:
                                     (h * NKB + kb) * 2 + 2]
                    regions = []
                    ent = pqkb.get(kb) if kb <= NPATCH - 1 else None
                    a = b = None
                    if ent is not None:
                        c0, c1, off = ent
                        a, b = c0, c1
                    if w0 > u0:
                        regions.append((u0, min(w0, u0 + 512), cta, None))
                    if a is not None and a < b:
                        regions.append((a, b, self.m8[:], (off, a)))
                    if w1 < u0 + 512:
                        regions.append((max(w1, u0), u0 + 512, ctb, None))
                    for (r0, r1, bias, pofs) in regions:
                        if pofs is not None:
                            off0, astart = pofs

                            def padd(e, r0=r0, r1=r1, off0=off0, u0=u0,
                                     scol=scol, h01=h01, slot=slot):
                                pc = scol + (r0 - u0)
                                return e.tensor_tensor(
                                    self.pbig[:, pc:pc + (r1 - r0)],
                                    self.pbig[:, pc:pc + (r1 - r0)],
                                    self.pq[h01][slot][:, off0:off0 + (r1 - r0)],
                                    ALU.add)

                            s.emit("vector", padd,
                                   reads=[("pb", sbank), ("pq", h01, slot)],
                                   writes=[("pb", sbank)])

                        def ex(e, r0=r0, r1=r1, bias=bias, u0=u0, scol=scol,
                               h01=h01, ptt=ptt):
                            pc = r0 - u0
                            return e.activation(
                                ptt[:, h01 * 512 + pc:h01 * 512 + pc + (r1 - r0)],
                                self.pbig[:, scol + pc:scol + pc + (r1 - r0)],
                                AF.Exp, bias=bias)

                        s.emit("scalar", ex,
                               reads=[("pb", sbank), ("ctabB",), ("m8",)],
                               writes=[("pt", par)])
                if kb > 1:
                    emit_av(kb - 2)
                step()
            emit_av(NKB - 2)
            emit_av(NKB - 1)
            step()
            for h01 in range(2):
                h = 2 * hp + h01
                s.emit("vector", lambda e, h01=h01: e.tensor_copy(
                    self.srow[:],
                    self.pbig[64:65, 1024 + h01 * 512:1536 + h01 * 512]),
                    reads=[("pb", 2 + h01), ("srow", "d")], writes=[("srow",)])
                self.dma("sync", io["sumsd"][h:h + 1, u0:u0 + 512], self.srow[:],
                         reads=[("srow",)], writes=[("srow", "d"), ("sumd", h01)])
                self.dma("sync", self.srq[h01][:],
                         bass.AP(io["sumsd"], h * TLOC + u0, [[4, 128], [1, 4]]),
                         reads=[("sumd", h01)], writes=[("srq", h01)])

                def ev(e, h01=h01):
                    return e.tensor_copy(
                        self.ahstg[h01][:],
                        self.pbig[0:64, 1024 + h01 * 512:1536 + h01 * 512])

                s.emit("vector", ev, reads=[("pb", 2 + h01), ("ahd", h01)],
                       writes=[("ahstg", h01)])

                def norm_g1(h01=h01, h=h, u0=u0):
                    s.emit("vector", lambda e: e.reciprocal(
                        self.srq[h01][:], self.srq[h01][:]),
                        reads=[("srq", h01)], writes=[("srq", h01)])
                    self.dma("sync",
                             bass.AP(io["recd"], h * TLOC + u0,
                                     [[4, 128], [1, 4]]),
                             self.srq[h01][:], reads=[("srq", h01)],
                             writes=[("recq", h01)])
                    self.dma("sync", self.rbc2[:],
                             bass.AP(io["recd"], h * TLOC + u0,
                                     [[0, 64], [1, 512]]),
                             reads=[("recq", h01), ("rbc2", "rd")],
                             writes=[("rbc2",)])

                def norm_g2(h01=h01, hp=hp, u0=u0):
                    s.emit("vector", lambda e: e.tensor_tensor(
                        self.ahstg[h01][:], self.ahstg[h01][:],
                        self.rbc2[:], ALU.mult),
                        reads=[("ahstg", h01), ("rbc2",)],
                        writes=[("ahstg", h01), ("rbc2", "rd")])
                    self.dma("sync",
                             io["ahd"][64 * h01:64 * h01 + 64,
                                       hp * TLOC + u0:hp * TLOC + u0 + 512],
                             self.ahstg[h01][:],
                             reads=[("ahstg", h01)], writes=[("ahd", h01)])

                self._norm_q.append(norm_g1)
                self._norm_q.append(norm_g2)

    def _p2_merged(self, pairs=range(8)):
        s = self.s
        self.ar_m.allocate()
        pairs = list(pairs)
        import itertools
        extra = {6: NE, 7: NE + 1}
        gens, nunits = {}, {}
        for hp in pairs:
            parts = [self._expert_units(2 * hp), self._expert_units(2 * hp + 1)]
            n = 132
            if hp in extra:
                parts.append(self._expert_units(extra[hp]))
                n = 198
            gens[hp] = itertools.chain(*parts)
            nunits[hp] = n

        NSLOTS = 122

        def make_step(gen, NUNITS):
            st = dict(slot=0, done=0)

            def step(k=1):
                if k > 1:
                    n = k
                else:
                    st["slot"] += 1
                    n = (st["slot"] * NUNITS) // NSLOTS - st["done"]
                for _ in range(max(0, n)):
                    try:
                        u = next(gen)
                    except StopIteration:
                        step.drained = True
                        return
                    u()
                    st["done"] += 1
            step.drained = False
            return step

        self._pq0_done = False
        import collections
        self._norm_q = collections.deque()
        PREPULL = 12
        for i, hp in enumerate(pairs):
            g = gens[hp]
            npre = 0
            for _ in range(PREPULL):
                try:
                    u = next(g)
                except StopIteration:
                    break
                u()
                npre += 1
            self._pair_setup(hp, lambda: None)
            if i + 1 < len(pairs):
                self.emit_wdmas(pairs[i + 1])
            nxt = pairs[i + 1] if i + 1 < len(pairs) else None
            step = make_step(gens[hp], nunits[hp])
            self._pair_heads(hp, step, nxt)
            step(200)
        while self._norm_q:
            self._norm_q.popleft()()
        self.s.phase()
        self.ar_m.free()

    def _p3_oproj_shared(self):
        nc, s, io = self.nc, self.s, self.io
        ar = nc.auto_arena()
        ahall = ar.sbuf_tensor("ahall", [128, 8 * TLOC], BF16)
        wot = ar.sbuf_tensor("wot", [128, 8 * 1024], BF16)
        stage2 = ar.sbuf_tensor("stage2", [128, 8 * 1024], BF16)
        ar.allocate()
        src = io["Wo"].rearrange("(fb p) c -> p fb c", p=128)
        dst3 = stage2.rearrange("p (fb c) -> p fb c", c=1024)
        self.dma("gpsimd", dst3[:], src, writes=[("stage2",)])
        for cb in range(8):
            self.dma("sync", ahall[:, cb * TLOC:(cb + 1) * TLOC],
                     io["ahd"][:, cb * TLOC:(cb + 1) * TLOC],
                     writes=[("ahall", cb)])
        for cb in range(8):
            for fb in range(8):
                pbi = (cb * 8 + fb) % 2
                pbk = self.pb_bf[pbi]

                def tr(e, cb=cb, fb=fb, pbk=pbk):
                    return e.transpose(pbk[:, 0:128],
                                       stage2[:, fb * 1024 + cb * 128:
                                              fb * 1024 + cb * 128 + 128],
                                       self.identb[:])

                s.emit("tensor", tr, reads=[("stage2",), ("identb",)],
                       writes=[("pb", pbi)])
                s.emit("vector", lambda e, cb=cb, fb=fb, pbk=pbk: e.tensor_copy(
                    wot[:, cb * 1024 + fb * 128:cb * 1024 + fb * 128 + 128],
                    pbk[:, 0:128]),
                    reads=[("pb", pbi)], writes=[("wot", cb)])
        for fb in range(8):
            for c in range(4):
                pbi = 2 + (fb * 4 + c) % 2
                pbk = self.pb[pbi]
                for cb in range(8):
                    def mm(e, cb=cb, fb=fb, c=c, pbk=pbk):
                        return e.matmul(
                            pbk[:], wot[:, cb * 1024 + fb * 128:
                                        cb * 1024 + fb * 128 + 128],
                            ahall[:, cb * TLOC + c * 512:cb * TLOC + c * 512 + 512],
                            start=(cb == 0), stop=(cb == 7))

                    s.emit("tensor", mm,
                           reads=[("wot", cb), ("ahall", cb)],
                           writes=[("pb", pbi)] if cb == 0 else (),
                           writes_accum=[("pb", pbi)] if cb > 0 else ())
                s.emit("vector", lambda e, fb=fb, c=c, pbk=pbk:
                       e.scalar_tensor_tensor(
                    self.acc[:, fb * TLOC + c * 512:fb * TLOC + c * 512 + 512],
                    pbk[:], self.b8["bo"][:, fb:fb + 1],
                    self.acc[:, fb * TLOC + c * 512:fb * TLOC + c * 512 + 512],
                    ALU.add, ALU.add),
                    reads=[("pb", pbi), ("acc", fb)], writes=[("acc", fb)])
        self.s.phase()
        ar.free()

    def _p6_final(self):
        nc, s, io = self.nc, self.s, self.io
        ar = nc.auto_arena()
        xb = [ar.sbuf_tensor(f"xb{i}", [128, D], F32) for i in range(3)]
        ob = [ar.sbuf_tensor(f"ob{i}", [128, D], F32) for i in range(2)]
        ar.allocate()
        for t0 in range(2):
            self.dma("sync", xb[t0 % 3][:], io["xrot"][t0 * 128:(t0 + 1) * 128, :],
                     writes=[("xb", t0 % 3)])
        for tt in range(16):
            par = tt % 3
            if tt + 2 < 16:
                self.dma("sync", xb[(tt + 2) % 3][:],
                         io["xrot"][(tt + 2) * 128:(tt + 3) * 128, :],
                         writes=[("xb", (tt + 2) % 3)])
            for fb in range(8):
                pbi = (fb + tt) % 2
                pbk = self.pb[pbi]

                def tr(e, tt=tt, fb=fb, pbk=pbk):
                    return e.transpose(
                        pbk[:, 0:128],
                        self.acc[:, fb * TLOC + tt * 128:fb * TLOC + tt * 128 + 128],
                        self.identf[:])

                s.emit("tensor", tr, reads=[("acc", fb), ("identf",)],
                       writes=[("pb", pbi)])

                opar = tt % 2

                def ev(e, tt=tt, fb=fb, par=par, opar=opar, pbk=pbk):
                    return e.tensor_tensor(ob[opar][:, fb * 128:(fb + 1) * 128],
                                           pbk[:, 0:128],
                                           xb[par][:, fb * 128:(fb + 1) * 128],
                                           ALU.add)

                s.emit("vector", ev, reads=[("pb", pbi), ("xb", par)],
                       writes=[("ob", opar, fb)])
            opar = tt % 2
            self.dma("sync", io["out"][tt * 128:(tt + 1) * 128, :], ob[opar][:],
                     reads=[("ob", opar, fb) for fb in range(8)])
        self.s.phase()
        ar.free()


def build_program(dbg=(), upto="full", pairs=range(8)):
    mk = MK(dbg=dbg)
    nc = mk.build(upto=upto, pairs=pairs)
    return nc, mk


def make_in_maps(inputs):
    x = np.ascontiguousarray(np.asarray(inputs["L0_attn_x"], dtype=np.float32))
    rel_bias = np.asarray(inputs["L0_attn_rel_bias"], dtype=np.float32)
    patches, ctab = host_tables(rel_bias)
    shared = {
        "Wq": inputs["L0_attn_Wq"], "Wk": inputs["L0_attn_Wk"],
        "Wv": inputs["L0_attn_Wv"], "Wo": inputs["L0_attn_Wo"],
        "bq": inputs["L0_attn_bq"], "bk": inputs["L0_attn_bk"],
        "bv": inputs["L0_attn_bv"], "bo": inputs["L0_attn_bo"],
        "gate_w": inputs["L0_ffn_gate_w"],
        "eg": np.asarray(inputs["L0_ffn_expert_gate"], dtype=np.float32)
            .reshape(NE, 8, 128, 4, 128).transpose(0, 3, 2, 1, 4),
        "eu": np.asarray(inputs["L0_ffn_expert_up"], dtype=np.float32)
            .reshape(NE, 8, 128, 4, 128).transpose(0, 3, 2, 1, 4),
        "ed": inputs["L0_ffn_expert_down"],
        "sge": np.stack([np.asarray(inputs["L0_ffn_shared_gate"],
                                    dtype=np.float32)[:, k * EI:(k + 1) * EI]
                         for k in range(2)])
            .reshape(2, 8, 128, 4, 128).transpose(0, 3, 2, 1, 4),
        "sue": np.stack([np.asarray(inputs["L0_ffn_shared_up"],
                                    dtype=np.float32)[:, k * EI:(k + 1) * EI]
                         for k in range(2)])
            .reshape(2, 8, 128, 4, 128).transpose(0, 3, 2, 1, 4),
        "sde": np.stack([np.asarray(inputs["L0_ffn_shared_down"],
                                    dtype=np.float32)[k * EI:(k + 1) * EI, :]
                         for k in range(2)]),
    }
    shared = {k: np.ascontiguousarray(np.asarray(v, dtype=np.float32))
              for k, v in shared.items()}
    in_maps = []
    for core in range(8):
        b, half = core // 2, core % 2
        qbase = half * TLOC
        xr = np.ascontiguousarray(
            np.concatenate([x[b, qbase:], x[b, :qbase]], axis=0))
        m = dict(shared)
        m["xrot"] = xr
        m["patches"] = np.ascontiguousarray(patches[half])
        m["ctab"] = np.ascontiguousarray(ctab[half])
        in_maps.append(m)
    return in_maps


def kernel(**inputs):
    nc, _ = build_program()
    in_maps = make_in_maps(inputs)
    res = run_bass_kernel_spmd(nc, in_maps, core_ids=list(range(8)))
    out = np.empty((B, T, D), dtype=np.float32)
    for core in range(8):
        b, half = core // 2, core % 2
        qbase = half * TLOC
        out[b, qbase:qbase + TLOC] = res.results[core]["out"]
    return out
